# version 17
# baseline (speedup 1.0000x reference)
import math
import numpy as np
import concourse.bass as bass
import concourse.mybir as mybir
from concourse.bass_utils import run_bass_kernel_spmd

F32 = mybir.dt.float32
F16 = mybir.dt.float16
BF16 = mybir.dt.bfloat16
AF = mybir.ActivationFunctionType
ALU = mybir.AluOpType

NEG_BIG = -60000.0
BAND_TH = 100.0


def alibi_slopes(n):
    def pow2_slopes(n):
        start = 2 ** (-(2 ** -(math.log2(n) - 3)))
        return [start * start ** i for i in range(n)]
    if math.log2(n).is_integer():
        return pow2_slopes(n)
    c = 2 ** math.floor(math.log2(n))
    return pow2_slopes(c) + alibi_slopes(2 * c)[::2][: n - c]


class Cfg:
    def __init__(self, S=4096, E=2048, I=8192):
        self.S, self.E, self.I = S, E, I
        self.QG = 256
        self.H = E // 128
        self.FT = E // 128
        self.IT = I // 128
        self.TOKF = S // 4
        self.NGRP = S // self.QG
        self.NSLOT = self.NGRP // 4
        self.SC = 1.0 / math.sqrt(128.0)
        self.CS = 32
        self.slopes = alibi_slopes(self.H)
        self.caps = [(4 * j + 4) * (self.QG // 128)
                     for j in range(self.NSLOT)]
        self.band = [int(math.ceil((BAND_TH / s + self.QG) / 128.0)) + 6
                     for s in self.slopes]
        self.W = [[min(self.caps[j], self.band[h]) for h in range(self.H)]
                  for j in range(self.NSLOT)]
        self.mw_w = [self.caps[j] * 128 + self.QG - 128
                     for j in range(self.NSLOT)]

    def grp(self, j, r):
        return 4 * j + (r if j % 2 == 0 else 3 - r)


def build_mwin(cfg, r):
    cols = []
    for j in range(cfg.NSLOT):
        g0 = cfg.grp(j, r) * cfg.QG
        w = np.arange(cfg.mw_w[j])[None, :]
        p = np.arange(128)[:, None]
        dist = g0 + w - (cfg.caps[j] - 1) * 128 - p
        cols.append(np.where(dist >= 0, -dist, NEG_BIG))
    return np.ascontiguousarray(np.concatenate(cols, axis=1).astype(np.float32))


def build_h128():
    h = np.array([[1.0]], dtype=np.float32)
    for _ in range(7):
        h = np.block([[h, h], [h, -h]])
    return h.astype(np.float32)


class Sched:
    ENGINES = ("sync", "gpsimd", "tensor", "vector", "scalar")

    def __init__(self, nc):
        self.nc = nc
        self.ops = {e: [] for e in self.ENGINES}
        self.sems = {}
        self.cnt = {}
        self.esem = {"tensor": "pe", "vector": "dve", "scalar": "act",
                     "gpsimd": "pool", "sync": "sp"}

    def sem(self, name):
        if name not in self.sems:
            self.sems[name] = self.nc.alloc_semaphore("s_" + name)
            self.cnt[name] = 0
        return self.sems[name]

    def op(self, eng, fn, waits=(), dma=None, inc=True):
        ev = None
        if dma is not None:
            self.sem(dma)
            self.cnt[dma] += 16
            ev = (dma, self.cnt[dma], 16)
        elif inc:
            s = self.esem[eng]
            self.sem(s)
            self.cnt[s] += 1
            ev = (s, self.cnt[s], 1)
        ws = tuple(w for w in waits if w is not None)
        self.ops[eng].append((fn, ws, ev))
        return ev

    def chain(self, eng_name, E, ins):
        s = self.sem(self.esem[eng_name] + "ch")
        self.cnt[self.esem[eng_name] + "ch"] += 1
        v = self.cnt[self.esem[eng_name] + "ch"]
        ins.then_inc(self.sems[self.esem[eng_name] + "ch"], 1)
        E.wait_ge(self.sems[self.esem[eng_name] + "ch"], v)

    def emit(self, blk):
        for eng in self.ENGINES:
            lst = self.ops[eng]
            if not lst:
                continue

            def body(E, lst=lst):
                done = {}
                for fn, ws, ev in lst:
                    need = {}
                    for (s, v, _b) in ws:
                        if v > need.get(s, 0):
                            need[s] = v
                    for s, v in need.items():
                        if v > done.get(s, 0):
                            E.wait_ge(self.sems[s], v)
                            done[s] = v
                    ins = fn(E)
                    if ev is not None:
                        ins.then_inc(self.sems[ev[0]], ev[2])
            getattr(blk, eng)(body)


def build_program(cfg):
    S, E, I, QG = cfg.S, cfg.E, cfg.I, cfg.QG
    FT, IT, H = cfg.FT, cfg.IT, cfg.H
    NSLOT, CS = cfg.NSLOT, cfg.CS
    TOKF = cfg.TOKF
    NSLAB_A = S // 128
    NSLAB_F = TOKF // 128
    NSLAB = NSLAB_A + NSLAB_F
    FH = TOKF // 2
    ITQ = IT // 4
    THN = max(1, TOKF // 512)
    TCH = TOKF // THN
    UPC = max(1, TOKF // 512)
    UCH = TOKF // UPC
    nst1 = int(math.log2(FT))
    nst2 = int(math.log2(IT))
    NSL = TOKF // CS
    KT = S // 128
    VCH = E // 512
    QCH = TOKF // 512 if TOKF >= 512 else 1
    QW = min(512, TOKF)
    MWTOT = sum(cfg.mw_w)

    nc = bass.Bass("TRN2", target_bir_lowering=False)
    din = lambda n, sh: nc.dram_tensor(n, sh, F32, kind="ExternalInput").ap()
    xT_d = din("xT", [E, S])
    xfT_d = din("xfT", [E, TOKF])
    wqT_d = din("wqT", [E, E])
    wkT_d = din("wkT", [E, E])
    wvT_d = din("wvT", [E, E])
    woT_d = din("woT", [E, E])
    bq_d = din("bq", [E])
    bk_d = din("bk", [E])
    bv_d = din("bv", [E])
    bo_d = din("bo", [E])
    slopes_d = din("slopes", [H])
    mwin_d = din("mwin", [128, MWTOT])
    su_ug_d = din("su_ug", [E])
    su_dp_d = din("su_dp", [I])
    wsc_d = din("wsc", [2 * I])
    wdp_d = din("wdp", [E])
    wupT_d = din("wupT", [E, I])
    wgT_d = din("wgT", [E, I])
    wdT_d = din("wdT", [I, E])
    h128_d = din("h128", [128, 128])
    out_d = nc.dram_tensor("out", [E, TOKF], F32, kind="ExternalOutput").ap()
    k_dram = nc.dram_tensor("k_scr", [H, 128, S], BF16, kind="Internal").ap()
    v_dram = nc.dram_tensor("v_scr", [S, E], BF16, kind="Internal").ap()
    s_dram = nc.dram_tensor("s_scr", [E, TOKF], BF16, kind="Internal").ap()

    BASE0 = ((int(nc.sbuf_base) + 63) // 64) * 64

    def at(name, shape, dt, o):
        return nc.alloc_sbuf_tensor_at(name, shape, dt, offset=BASE0 + o).ap()

    _o = [0]
    def cst(name, shape, dt):
        o = (_o[0] + 63) // 64 * 64
        ap = at(name, shape, dt, o)
        _o[0] = o + int(np.prod(shape[1:])) * mybir.dt.size(dt)
        return ap

    ones1_b = cst("ones1_b", [1, 128], BF16)
    ones128_b = cst("ones128_b", [128, 1], BF16)
    ones1_f = cst("ones1_f", [1, 128], F32)
    h128_b = cst("h128_b", [128, 128], BF16)
    su_ug_l = cst("su_ug_l", [128, FT], F32)
    su_dp_l = cst("su_dp_l", [128, IT], F32)
    upvec = cst("upvec", [128, IT], F32)
    gatevec = cst("gatevec", [128, IT], F32)
    dnvec = cst("dnvec", [128, FT], F32)
    bq_l = cst("bq_l", [128, H], F32)
    bk_l = cst("bk_l", [128, H], F32)
    bo_l = cst("bo_l", [128, FT], F32)
    slopes_r = cst("slopes_r", [1, H], F32)
    slopes_l = cst("slopes_l", [128, H], F32)
    bvB = cst("bvB", [128, E], BF16)
    m_row = cst("m_row", [1, 256], F32)
    sq_row = cst("sq_row", [1, 256], F32)
    msq_row = cst("msq_row", [1, 256], F32)
    rstd_row = cst("rstd_row", [1, 256], F32)
    var_row = cst("var_row", [1, 256], F32)
    mrstd_row = cst("mrstd_row", [1, 256], F32)
    rstdB = cst("rstdB", [128, 256], F32)
    mrstdB = cst("mrstdB", [128, 256], F32)
    xn_scr = cst("xn_scr", [128, 128], F32)
    kstg = [cst(f"kstg{i}", [128, H * 128], BF16) for i in range(2)]
    vstg = [cst(f"vstg{i}", [128, 512], BF16) for i in range(2)]
    sstg = [cst(f"sstg{i}", [128, QG], BF16) for i in range(2)]
    R0 = ((_o[0] + 63) // 64) * 64

    sz_U4 = ITQ * TOKF * 2
    szA = max(FT * E * 2, MWTOT * 2 + 4 * S * 2 + 4 * QG * 2 + 4 * QG * 4,
              2 * FT * FH * 4, 2 * sz_U4)
    szB = max(FT * E * 2, H * TOKF * 2 + H * NSLOT * QG * 2, 2 * sz_U4)
    e_stage = 2 * IT * CS * 4 + 4 * FH * 2 + 6 * FH * 4
    szC = max(FT * TOKF * 2, e_stage)
    szD = max(5 * FT * 256,
              E * 4,
              3 * FT * 256 + 2 * TOKF * 4,
              3 * 1024)
    A = R0
    B = A + szA
    Cr = B + szB
    D = Cr + szC
    TOP = D + szD
    LIMIT = 229300 - BASE0 - 1024
    assert TOP <= LIMIT, (
        f"sbuf over: {TOP} > {LIMIT} (A {szA} B {szB} C {szC} D {szD} R0 {R0})")

    wk_b = at("wk_b", [128, FT * E], BF16, A)
    wq_b = at("wq_b", [128, FT * E], BF16, A)
    wo_b = at("wo_b", [128, FT * E], BF16, A)
    mwin_s = at("mwin_s", [128, MWTOT], F16, A)
    kh = [at(f"kh{i}", [128, S], BF16, A + MWTOT * 2 + i * S * 2)
          for i in range(2)]
    vh = [at(f"vh{i}", [128, KT * 128], BF16, A + MWTOT * 2 + (2 + i) * S * 2)
          for i in range(2)]
    ETs = [at(f"ET{i}", [128, QG], BF16, A + MWTOT * 2 + 4 * S * 2 + i * QG * 2)
           for i in range(4)]
    _pA = A + MWTOT * 2 + 4 * S * 2 + 4 * QG * 2
    rinv2 = at("rinv2", [1, 2 * QG], F32, _pA)
    rbB2 = at("rbB2", [128, 2 * QG], F32, _pA + 2 * QG * 4)
    assert _pA + 2 * QG * 4 + 2 * QG * 4 <= A + szA
    Fbuf = at("Fbuf", [128, FT * FH], F32, A)
    F2buf = at("F2buf", [128, FT * FH], F32, A + FT * FH * 4)
    F2b = at("F2b", [128, FT * FH], BF16,
             A if nst1 % 2 == 0 else A + FT * FH * 4)
    wv_b = at("wv_b", [128, FT * E], BF16, B)
    q_sb = at("q_sb", [128, H * TOKF], BF16, B)
    OT_sb = at("OT_sb", [128, H * NSLOT * QG], BF16, B + H * TOKF * 2)
    Us = [at("U0", [128, ITQ * TOKF], BF16, A),
          at("U1", [128, ITQ * TOKF], BF16, A + sz_U4),
          at("U2", [128, ITQ * TOKF], BF16, B),
          at("U3", [128, ITQ * TOKF], BF16, B + sz_U4)]
    xnTf = at("xnTf", [128, FT * TOKF], BF16, Cr)
    C1 = at("C1", [128, IT * CS], F32, Cr)
    C2 = at("C2", [128, IT * CS], F32, Cr + IT * CS * 4)
    p4c = Cr + 2 * IT * CS * 4
    g_st = at("g_st", [128, 2 * FH], BF16, p4c)
    ss_st = at("ss_st", [128, 2 * FH], BF16, p4c + FH * 4)
    xf_st = at("xf_st", [128, 2 * FH], F32, p4c + FH * 8)
    o_st = at("o_st", [128, 2 * FH], F32, p4c + FH * 16)
    res_st = at("res_st", [128, 2 * FH], F32, p4c + FH * 24)
    xTb = [at(f"xTb{i}", [128, FT * 128], BF16, D + i * FT * 256)
           for i in range(2)]
    xnT2 = [at(f"xnT2{i}", [128, FT * 128], BF16, D + (2 + i) * FT * 256)
            for i in range(2)]
    sq_scr = at("sq_scr", [128, FT * 128], BF16, D + 4 * FT * 256)
    bv_r = at("bv_r", [1, E], F32, D)
    wring = [at(f"wring{i}", [128, FT * 128], BF16, D + i * FT * 256)
             for i in range(3)]
    A_st = at("A_st", [128, TOKF], F32, D + 3 * FT * 256)
    B_st = at("B_st", [128, TOKF], F32, D + 3 * FT * 256 + TOKF * 4)
    K2 = next(k for k in (4, 2, 1)
              if IT % k == 0 and 4 * k * 1024 <= szD)
    NWDR = 5 if 5 * K2 * 1024 <= szD else 4
    wdring = [at(f"wdring{i}", [128, K2 * 512], BF16, D + i * K2 * 1024)
              for i in range(NWDR)]
    assert NWDR * K2 * 1024 <= szD

    def Uap(it, c0, w):
        t = Us[it // ITQ]
        base = (it % ITQ) * TOKF
        return t[:, base + c0: base + c0 + w]

    pb = [nc.alloc_psum_tensor(f"pb{i}", [128, 512], F32).ap() for i in range(8)]
    nc.sbuf_base = BASE0 + ((TOP + 63) // 64) * 64

    sch = Sched(nc)
    OP = sch.op

    c0_evs = []
    def mk_cdma(dst, src):
        return lambda eng, dst=dst, src=src: eng.dma_start(
            dst[:], src, allow_slow_non_contiguous=True)
    for dst, src in [
        (su_ug_l, su_ug_d.rearrange("(t p) -> p t", p=128)),
        (su_dp_l, su_dp_d.rearrange("(t p) -> p t", p=128)),
        (upvec, wsc_d[0:I].rearrange("(t p) -> p t", p=128)),
        (gatevec, wsc_d[I:2 * I].rearrange("(t p) -> p t", p=128)),
        (dnvec, wdp_d.rearrange("(t p) -> p t", p=128)),
        (bq_l, bq_d.rearrange("(t p) -> p t", p=128)),
        (bk_l, bk_d.rearrange("(t p) -> p t", p=128)),
        (bo_l, bo_d.rearrange("(t p) -> p t", p=128)),
        (slopes_r, slopes_d[None, :]),
        (bv_r, bv_d[None, :]),
    ]:
        c0_evs.append(OP("sync", mk_cdma(dst, src), dma="c0"))
    ev_c0 = c0_evs[-1]

    ev_h128 = OP("gpsimd", lambda eng: eng.dma_start(h128_b[:], h128_d[:]),
                 dma="cg")
    OP("gpsimd", lambda eng: eng.memset(ones1_b[:], 1.0), inc=False)
    OP("gpsimd", lambda eng: eng.memset(ones128_b[:], 1.0), inc=False)
    ev_ones = OP("gpsimd", lambda eng: eng.memset(ones1_f[:], 1.0))
    ev_wk = OP("gpsimd", lambda eng: eng.dma_start(
        wk_b.rearrange("p (t d) -> p t d", t=FT),
        wkT_d.rearrange("(t p) d -> p t d", p=128)), dma="wkd")
    ev_wv = OP("gpsimd", lambda eng: eng.dma_start(
        wv_b.rearrange("p (t d) -> p t d", t=FT),
        wvT_d.rearrange("(t p) d -> p t d", p=128)), dma="wvd")

    def p0_vec(eng):
        inv1 = 1.0 / math.sqrt(E)
        sch.chain("vector", eng,
                  eng.tensor_scalar(upvec[:], upvec[:], inv1, None, ALU.mult))
        sch.chain("vector", eng,
                  eng.tensor_mul(upvec[:], upvec[:], su_dp_l[:]))
        eng.tensor_scalar(gatevec[:], gatevec[:], inv1, None, ALU.mult)
        eng.tensor_scalar(dnvec[:], dnvec[:], 1.0 / math.sqrt(I), None,
                          ALU.mult)
        return eng.tensor_scalar(bq_l[:], bq_l[:], cfg.SC, None, ALU.mult)
    ev_vec0 = OP("vector", p0_vec, waits=c0_evs)

    ev_pe0 = OP("tensor", (lambda eng: eng.matmul(
        pb[6][:, 0:H], ones1_f[:], slopes_r[:], start=True, stop=True)),
        waits=[ev_c0, ev_ones] + c0_evs[-2:])
    ev_slo = OP("scalar", (lambda eng: eng.copy(slopes_l[:], pb[6][:, 0:H])),
                waits=[ev_pe0])

    ev_bvB = None
    for ch in range(E // 512):
        ev_m = OP("tensor", (lambda eng, ch=ch: eng.matmul(
            pb[7][:, 0:512], ones1_f[:], bv_r[0:1, ch * 512:(ch + 1) * 512],
            start=True, stop=True)), waits=[ev_pe0, ev_bvB])
        ev_bvB = OP("scalar", (lambda eng, ch=ch: eng.copy(
            bvB[:, ch * 512:(ch + 1) * 512], pb[7][:, 0:512])), waits=[ev_m])

    ev_ld = {}
    ev_bcE = {}
    ev_sq = {}
    ev_sqmm = {}
    ev_rows = {}
    ev_rstd = {}
    ev_bcast_ev = {}
    ev_cons_all = {}
    ev_xnt = {}
    ev_kdma = {}
    ev_vdma = {}
    ev_kevac = {}
    ev_vevac = {}

    emit_order = []

    def ln_head(slab, li):
        is_ffn = slab >= NSLAB_A
        cc0 = (slab - NSLAB_A) * 128 if is_ffn else slab * 128
        src_t = xfT_d if is_ffn else xT_d
        par = li % 2
        psem = "xt%d" % par
        prev = emit_order[li - 1] if li >= 1 else None
        pprev = emit_order[li - 2] if li >= 2 else None

        ev_ld[slab] = OP("gpsimd", (lambda eng, s=src_t, c=cc0, p=par:
            eng.dma_start(xTb[p].rearrange("p (t c) -> p t c", t=FT),
                          s[:, c:c + 128].rearrange("(t p) c -> p t c",
                                                    p=128))),
            waits=[ev_xnt.get(pprev),
                   ev_bvB if li <= 1 else None], dma=psem)

        ev_sq[slab] = OP("vector", (lambda eng, par=par: eng.tensor_mul(
            sq_scr[:], xTb[par][:], xTb[par][:])),
            waits=[ev_ld[slab], ev_sqmm.get(prev)])

        def p1_stats(eng, par=par, ev_s=ev_sq[slab]):
            for ft in range(FT):
                eng.matmul(pb[0][0:1, 0:128], ones128_b[:],
                           xTb[par][:, ft * 128:ft * 128 + 128],
                           start=(ft == 0), stop=(ft == FT - 1))
            eng.wait_ge(sch.sems[ev_s[0]], ev_s[1])
            for ft in range(FT):
                ins = eng.matmul(pb[0][0:1, 128:256], ones128_b[:],
                                 sq_scr[:, ft * 128:ft * 128 + 128],
                                 start=(ft == 0), stop=(ft == FT - 1))
            return ins
        ev_sqmm[slab] = OP("tensor", p1_stats,
                           waits=[ev_ld[slab], ev_rows.get(prev)])

        pr = lambda t, q=par: t[0:1, q * 128:q * 128 + 128]

        def p1_rows(eng, par=par):
            eng.activation(pr(m_row), pb[0][0:1, 0:128], AF.Copy,
                           scale=1.0 / E)
            return eng.activation(pr(sq_row), pb[0][0:1, 128:256], AF.Copy,
                                  scale=1.0 / E)
        ev_rows[slab] = OP("scalar", p1_rows,
                           waits=[ev_sqmm[slab], ev_rstd.get(pprev)])

        def p1_var(eng, par=par):
            sch.chain("vector", eng,
                      eng.tensor_mul(pr(msq_row), pr(m_row), pr(m_row)))
            return eng.scalar_tensor_tensor(pr(var_row), pr(sq_row), 1e-5,
                                            pr(msq_row), ALU.add,
                                            ALU.subtract)
        ev_var = OP("vector", p1_var, waits=[ev_rows[slab]])

        ev_sd = OP("scalar", (lambda eng, par=par: eng.activation(
            pr(msq_row), pr(var_row), AF.Sqrt)), waits=[ev_var])

        def p1_rstd(eng, par=par):
            sch.chain("vector", eng,
                      eng.reciprocal(pr(rstd_row), pr(msq_row)))
            return eng.tensor_mul(pr(mrstd_row), pr(m_row), pr(rstd_row))
        ev_rstd[slab] = OP("vector", p1_rstd,
                           waits=[ev_sd, ev_bcE.get(pprev)])

    def ln_tail(slab, li):
        is_ffn = slab >= NSLAB_A
        cc0 = (slab - NSLAB_A) * 128 if is_ffn else slab * 128
        par = li % 2
        prev = emit_order[li - 1] if li >= 1 else None
        pprev = emit_order[li - 2] if li >= 2 else None
        pr = lambda t, q=par: t[0:1, q * 128:q * 128 + 128]

        def p1_bc(eng, par=par):
            eng.matmul(pb[1][:, 0:128], ones1_f[:], pr(rstd_row), start=True,
                       stop=True)
            return eng.matmul(pb[1][:, 128:256], ones1_f[:], pr(mrstd_row),
                              start=True, stop=True)
        ev_bc = OP("tensor", p1_bc,
                   waits=[ev_rstd[slab], ev_bcast_ev.get(prev), ev_ones])
        ev_bcE[slab] = ev_bc

        def p1_bcev(eng, par=par):
            eng.copy(rstdB[:, par * 128:par * 128 + 128], pb[1][:, 0:128])
            return eng.copy(mrstdB[:, par * 128:par * 128 + 128],
                            pb[1][:, 128:256])
        ev_bcast_ev[slab] = OP("scalar", p1_bcev,
                               waits=[ev_bc, ev_xnt.get(pprev)])

        def p1_xnt(eng, par=par, is_ffn=is_ffn, cc0=cc0):
            for ft in range(FT):
                sch.chain("vector", eng, eng.tensor_mul(
                    xn_scr[:], xTb[par][:, ft * 128:ft * 128 + 128],
                    rstdB[:, par * 128:par * 128 + 128]))
                dst = (xnTf[:, ft * TOKF + cc0: ft * TOKF + cc0 + 128]
                       if is_ffn else xnT2[par][:, ft * 128:ft * 128 + 128])
                ins = eng.tensor_sub(dst, xn_scr[:],
                                     mrstdB[:, par * 128:par * 128 + 128])
                if ft < FT - 1:
                    sch.chain("vector", eng, ins)
            return ins
        ev_xnt[slab] = OP("vector", p1_xnt,
                          waits=[ev_bcast_ev[slab], ev_vec0,
                                 ev_xnt.get(prev),
                                 ev_cons_all.get(slab - 2)
                                 if not is_ffn else None])
        if is_ffn:
            ev_cons_all[slab] = ev_sqmm[slab]

    def emit_kv(slab, par2, tail_cb):
        par = slab % 2
        ev_kc = {}
        for hch in range(H):
            def p1_k(eng, hch=hch, par2=par2):
                bank = pb[2 + (hch % 2)]
                for ft in range(FT):
                    ins = eng.matmul(
                        bank[:, 0:128],
                        wk_b[:, (ft * H + hch) * 128:(ft * H + hch) * 128 + 128],
                        xnT2[par2][:, ft * 128:ft * 128 + 128],
                        start=(ft == 0), stop=(ft == FT - 1))
                return ins
            wprev = (ev_kevac.get((slab, hch - 2)) if hch >= 2
                     else ev_kevac.get((slab - 1, H - 2 + (hch % 2))))
            ev_kc[hch] = OP("tensor", p1_k,
                            waits=[ev_xnt[slab], ev_wk, wprev])
            if hch == 1:
                tail_cb()

            def p1_kev(eng, hch=hch, par=par):
                bank = pb[2 + (hch % 2)]
                return eng.activation(
                    kstg[par][:, hch * 128:hch * 128 + 128],
                    bank[:, 0:128], AF.Identity,
                    bias=bk_l[:, hch:hch + 1])
            ev_kevac[(slab, hch)] = OP(
                "scalar", p1_kev,
                waits=[ev_kc[hch],
                       ev_kdma.get(slab - 2) if hch == 0 else None])

        def p1_kdma(eng, slab=slab, par=par):
            dst = k_dram[:, :, slab * 128:slab * 128 + 128].rearrange(
                "h d t -> d h t")
            return eng.dma_start(dst,
                                 kstg[par].rearrange("p (h t) -> p h t",
                                                     h=H))
        ev_kdma[slab] = OP("scalar", p1_kdma,
                           waits=[ev_kevac[(slab, H - 1)]],
                           dma="kd%d" % par)

        ev_vc = {}
        for vch in range(VCH):
            def p1_v(eng, vch=vch, par2=par2):
                bank = pb[4 + (vch % 2)]
                for ft in range(FT):
                    ins = eng.matmul(
                        bank[:, 0:512],
                        xnT2[par2][:, ft * 128:ft * 128 + 128],
                        wv_b[:, ft * E + vch * 512:ft * E + vch * 512 + 512],
                        start=(ft == 0), stop=(ft == FT - 1))
                return ins
            wprev = (ev_vevac.get((slab, vch - 2)) if vch >= 2
                     else ev_vevac.get((slab - 1, VCH - 2 + (vch % 2))))
            ev_vc[vch] = OP("tensor", p1_v,
                            waits=[ev_xnt[slab], ev_wv, wprev])

            def p1_vev(eng, vch=vch):
                bank = pb[4 + (vch % 2)]
                return eng.tensor_add(vstg[vch % 2][:], bank[:, 0:512],
                                      bvB[:, vch * 512:vch * 512 + 512])
            ev_vevac[(slab, vch)] = OP(
                "vector", p1_vev,
                waits=[ev_vc[vch], ev_bvB,
                       ev_vdma.get((slab, vch - 2)),
                       ev_vdma.get((slab - 1, VCH - 2 + (vch % 2)))])

            def p1_vdma(eng, slab=slab, vch=vch):
                dst = v_dram[slab * 128:slab * 128 + 128,
                             vch * 512:vch * 512 + 512]
                return eng.dma_start(dst, vstg[vch % 2][:])
            ev_vdma[(slab, vch)] = OP("scalar", p1_vdma,
                                      waits=[ev_vevac[(slab, vch)]],
                                      dma="vd%d" % (vch % 2))
        ev_cons_all[slab] = ev_vc[VCH - 1]

    step = NSLAB_A // max(1, NSLAB_F // 2)
    fi = NSLAB_A
    for a in range(NSLAB_A):
        emit_order.append(a)
        if (a + 1) % step == 0 and fi < NSLAB:
            emit_order.append(fi)
            emit_order.append(fi + 1)
            fi += 2
    while fi < NSLAB:
        emit_order.append(fi)
        fi += 1
    pos_attn = {s: p for p, s in enumerate(emit_order) if s < NSLAB_A}
    pos_of = {s: p for p, s in enumerate(emit_order)}

    tailq = []
    li = [0]

    def pop_tail():
        s, lpos = tailq.pop(0)
        ln_tail(s, lpos)

    def pump(limit):
        while li[0] < NSLAB and li[0] <= limit:
            while tailq and tailq[0][1] <= li[0] - 2:
                pop_tail()
            s = emit_order[li[0]]
            ln_head(s, li[0])
            tailq.append((s, li[0]))
            li[0] += 1

    pump(1)
    pop_tail()
    for ki in range(NSLAB_A):
        pump(min(NSLAB - 1, pos_attn[ki] + 2))

        def tail_cb(ki=ki):
            while tailq and (tailq[0][0] >= NSLAB_A
                             or tailq[0][0] <= ki + 1):
                pop_tail()
        emit_kv(ki, pos_attn[ki] % 2, tail_cb)
    pump(NSLAB - 1)
    while tailq:
        pop_tail()

    ev_p1_pe = ev_cons_all[NSLAB - 1]
    ev_spills = [(s, sch.cnt[s], 16) for s in ("kd0", "kd1", "vd0", "vd1")]

    ev_wq = OP("gpsimd", lambda eng: eng.dma_start(
        wq_b.rearrange("p (t d) -> p t d", t=FT),
        wqT_d.rearrange("(t p) d -> p t d", p=128)),
        waits=[ev_cons_all[NSLAB_A - 1]], dma="wqd")

    ev_qc = {}
    ev_qevac = {}
    for hch in range(H):
        for u in range(QCH):
            k2 = hch * QCH + u

            def p2_q(eng, hch=hch, u=u, k2=k2):
                bank = pb[2 + (k2 % 2)]
                for ft in range(FT):
                    ins = eng.matmul(
                        bank[:, 0:QW],
                        wq_b[:, (ft * H + hch) * 128:(ft * H + hch) * 128 + 128],
                        xnTf[:, ft * TOKF + u * QW:ft * TOKF + u * QW + QW],
                        start=(ft == 0), stop=(ft == FT - 1))
                return ins
            xn_need = NSLAB_A + min(NSLAB_F, 4 * (u + 1)) - 1
            lastk = H - 2 + ((H - 2) % 2 != k2 % 2)
            ev_qc[k2] = OP("tensor", p2_q,
                           waits=[ev_wq, ev_xnt[xn_need],
                                  ev_qevac.get(k2 - 2),
                                  ev_kevac[(NSLAB_A - 1, lastk)]
                                  if k2 <= 1 else None])

            def p2_qe(eng, hch=hch, u=u, k2=k2):
                bank = pb[2 + (k2 % 2)]
                return eng.activation(
                    q_sb[:, hch * TOKF + u * QW:hch * TOKF + u * QW + QW],
                    bank[:, 0:QW], AF.Identity, bias=bq_l[:, hch:hch + 1],
                    scale=cfg.SC)
            ev_qevac[k2] = OP("scalar", p2_qe, waits=[ev_qc[k2]])
    ev_q_done = ev_qevac[H * QCH - 1]

    ev_mw = OP("gpsimd", lambda eng: eng.dma_start(mwin_s[:], mwin_d[:]),
               waits=[ev_q_done], dma="mwd")

    mw_base = []
    acc = 0
    for j in range(NSLOT):
        mw_base.append(acc)
        acc += cfg.mw_w[j]

    PV_LA = 3
    bank_last_exp = [None] * 4
    et_last_pv = [None] * 4
    ev_khl = {}
    ev_vhl = {}
    ev_head_done = {}
    ev_evac = {}
    ev_bcast = {}
    gidx = 0
    hj = 0
    pn = None

    def norm_recip(pn):
        _hj, pvl, pvb, par = pn
        return OP("vector", (lambda eng, par=par: eng.reciprocal(
            rinv2[0:1, par * QG:par * QG + QG], pb[6 + par][0:1, 0:QG])),
            waits=[pvl, ev_bcast.get(pn[0] - 2)])

    def norm_bcast(pn, ev_ri):
        _hj, pvl, pvb, par = pn
        ev_bcast[_hj] = OP("tensor", (lambda eng, par=par:
            eng.matmul(pb[6 + par][:, 0:QG], ones1_f[:],
                       rinv2[0:1, par * QG:par * QG + QG],
                       start=True, stop=True)), waits=[ev_ri])
        return ev_bcast[_hj]

    def norm_rbc(pn, ev_rb):
        _hj, pvl, pvb, par = pn
        return OP("scalar", (lambda eng, par=par: eng.copy(
            rbB2[:, par * QG:par * QG + QG], pb[6 + par][:, 0:QG])),
            waits=[ev_rb])

    def norm_evac(pn, ev_rc, oslot):
        _hj, pvl, pvb, par = pn
        ev_evac[_hj] = OP("vector", (lambda eng, pvb=pvb, par=par,
                                     oslot=oslot:
            eng.tensor_mul(OT_sb[:, oslot * QG:oslot * QG + QG],
                           pvb[:, 0:QG], rbB2[:, par * QG:par * QG + QG])),
            waits=[ev_rc])
        return ev_evac[_hj]

    oslot_of = {}
    for h in range(H):
        hp = h % 2
        ev_khl[h] = OP("sync", (lambda eng, h=h, hp=hp: eng.dma_start(
            kh[hp][:], k_dram[h, :, :])),
            waits=ev_spills + [ev_head_done.get(h - 2),
                   ev_mw if h <= 1 else None], dma="kh%d" % hp)
        ev_vhl[h] = OP("sync", (lambda eng, h=h, hp=hp: eng.dma_start(
            vh[hp].rearrange("p (t d) -> p t d", t=KT),
            v_dram[:, h * 128:h * 128 + 128].rearrange("(t p) d -> p t d",
                                                       p=128))),
            waits=ev_spills + [ev_head_done.get(h - 2),
                   ev_mw if h <= 1 else None], dma="vh%d" % hp)

        for j in range(NSLOT):
            cap = cfg.caps[j]
            W = cfg.W[j][h]
            assert W > PV_LA + 2
            q_ap = q_sb[:, h * TOKF + j * QG: h * TOKF + j * QG + QG]
            par = hj % 2
            pvb = pb[4 + par]
            oslot_of[hj] = h * NSLOT + j

            ev_st = {}
            ev_al = {}
            ev_ex = {}
            ev_pv = {}
            ev_ri_p = None
            g0 = gidx

            def emit_pv(idx, g0=g0, cap=cap, W=W, hp=hp, pvb=pvb, h=h):
                rho = cap - W + idx
                e4 = (g0 + idx) % 4

                def fn(eng, rho=rho, e4=e4, idx=idx, W=W, hp=hp, pvb=pvb,
                       par=par):
                    eng.matmul(pvb[:, 0:QG],
                               vh[hp][:, rho * 128:rho * 128 + 128],
                               ETs[e4][:], start=(idx == 0),
                               stop=(idx == W - 1))
                    return eng.matmul(pb[6 + par][0:1, 0:QG], ones128_b[:],
                                      ETs[e4][:], start=(idx == 0),
                                      stop=(idx == W - 1))
                ev_pv[idx] = OP("tensor", fn,
                                waits=[ev_ex[idx],
                                       ev_evac.get(hj - 2) if idx == 0
                                       else None,
                                       ev_vhl[h] if idx == 0 else None])
                et_last_pv[e4] = ev_pv[idx]

            for idx in range(W):
                rho = cap - W + idx
                b4 = gidx % 4
                e4 = gidx % 4

                def mk_st(rho=rho, b4=b4, q_ap=q_ap, hp=hp):
                    def fn(eng):
                        return eng.matmul(pb[b4][:, 0:QG],
                                          kh[hp][:, rho * 128:rho * 128 + 128],
                                          q_ap, start=True, stop=True)
                    return fn
                ev_st[idx] = OP("tensor", mk_st(),
                                waits=[bank_last_exp[b4],
                                       ev_khl[h] if idx == 0 else None,
                                       ev_q_done if hj == 0 and idx == 0
                                       else None])

                coff = mw_base[j] + (cap - 1 - rho) * 128
                ev_al[idx] = OP("vector", (lambda eng, b4=b4, coff=coff, h=h:
                    eng.scalar_tensor_tensor(
                        pb[b4][:, 0:QG], mwin_s[:, coff:coff + QG],
                        slopes_l[:, h:h + 1], pb[b4][:, 0:QG],
                        ALU.mult, ALU.add)),
                    waits=[ev_st[idx], ev_mw])

                if idx == 1 and pn is not None:
                    ev_ri_p = norm_recip(pn)

                ev_ex[idx] = OP("scalar", (lambda eng, b4=b4, e4=e4:
                    eng.activation(ETs[e4][:], pb[b4][:, 0:QG], AF.Exp)),
                    waits=[ev_al[idx], et_last_pv[e4]])
                bank_last_exp[b4] = ev_ex[idx]

                if idx == 2 and pn is not None:
                    ev_rb_p = norm_bcast(pn, ev_ri_p)
                if idx == 3 and pn is not None:
                    ev_rc_p = norm_rbc(pn, ev_rb_p)
                if idx == 4 and pn is not None:
                    norm_evac(pn, ev_rc_p, oslot_of[pn[0]])
                    pn = None

                if idx >= PV_LA:
                    emit_pv(idx - PV_LA)
                gidx += 1

            for idx in range(W - PV_LA, W):
                emit_pv(idx)

            pn = (hj, ev_pv[W - 1], pvb, par)
            hj += 1
        ev_head_done[h] = pn[1]

    ev_ri_p = norm_recip(pn)
    ev_rb_p = norm_bcast(pn, ev_ri_p)
    ev_rc_p = norm_rbc(pn, ev_rb_p)
    ev_last_evac = norm_evac(pn, ev_rc_p, oslot_of[pn[0]])

    ev_wo = OP("gpsimd", lambda eng: eng.dma_start(
        wo_b.rearrange("p (t d) -> p t d", t=FT),
        woT_d.rearrange("(t p) d -> p t d", p=128)),
        waits=[ev_last_evac], dma="wod")

    ev_wo_mm = {}
    ev_wo_ev = {}
    ev_sdma = {}
    for j in range(NSLOT):
        for eo in range(FT):
            kk = j * FT + eo
            par = kk % 2

            def p3_wo(eng, j=j, eo=eo, par=par):
                for ein in range(H):
                    ins = eng.matmul(
                        pb[6 + par][:, 0:QG],
                        wo_b[:, (ein * FT + eo) * 128:(ein * FT + eo) * 128 + 128],
                        OT_sb[:, (ein * NSLOT + j) * QG:(ein * NSLOT + j) * QG + QG],
                        start=(ein == 0), stop=(ein == H - 1))
                return ins
            ev_wo_mm[kk] = OP("tensor", p3_wo,
                              waits=[ev_last_evac, ev_wo,
                                     ev_wo_ev.get(kk - 2)])

            ev_wo_ev[kk] = OP("scalar", (lambda eng, eo=eo, par=par:
                eng.activation(sstg[par][:], pb[6 + par][:, 0:QG],
                               AF.Identity, bias=bo_l[:, eo:eo + 1])),
                waits=[ev_wo_mm[kk], ev_sdma.get(kk - 2)])

            ev_sdma[kk] = OP("scalar", (lambda eng, j=j, eo=eo, par=par:
                eng.dma_start(s_dram[eo * 128:eo * 128 + 128,
                                     j * QG:j * QG + QG], sstg[par][:])),
                waits=[ev_wo_ev[kk]], dma="sd%d" % par)
    ev_sd_all = [("sd0", sch.cnt["sd0"], 16), ("sd1", sch.cnt["sd1"], 16)]
    ev_attn_pe = ev_wo_mm[NSLOT * FT - 1]

    ev_xh_evac = {}
    ev_h128mm_last = None
    nch = max(1, FH // 512)
    wch = FH // nch
    for hh in range(2):
        def p4_su(eng, hh=hh):
            for ft in range(FT):
                ins = eng.tensor_scalar(
                    Fbuf[:, ft * FH:(ft + 1) * FH],
                    xnTf[:, ft * TOKF + hh * FH:ft * TOKF + hh * FH + FH],
                    su_ug_l[:, ft:ft + 1], None, ALU.mult)
            return ins
        ev_su = OP("vector", p4_su,
                   waits=[ev_attn_pe, ev_xnt.get(NSLAB - 1), ev_h128mm_last])

        ev_stg = ev_su
        src, dst = Fbuf, F2buf
        for st in range(nst1):
            d = 1 << st
            dd = F2b if st == nst1 - 1 else dst

            def p4_stage(eng, src=src, dd=dd, d=d):
                s4 = src.rearrange("p (g two w) -> p g two w", two=2, w=d * FH)
                d4 = dd.rearrange("p (g two w) -> p g two w", two=2, w=d * FH)
                eng.tensor_add(d4[:, :, 0, :], s4[:, :, 0, :], s4[:, :, 1, :])
                return eng.tensor_sub(d4[:, :, 1, :], s4[:, :, 0, :],
                                      s4[:, :, 1, :])
            ev_stg = OP("vector", p4_stage, waits=[ev_stg])
            src, dst = dst, src

        for ft in range(FT):
            for chi in range(nch):
                k = ft * nch + chi
                gk = hh * FT * nch + k

                def p4_h1(eng, ft=ft, chi=chi, k=k):
                    return eng.matmul(
                        pb[k % 2][:, 0:wch], h128_b[:],
                        F2b[:, ft * FH + chi * wch:ft * FH + chi * wch + wch],
                        start=True, stop=True)
                ev_mm = OP("tensor", p4_h1,
                           waits=[ev_stg, ev_h128, ev_xh_evac.get(gk - 2)])

                def p4_h1e(eng, ft=ft, chi=chi, k=k, hh=hh):
                    return eng.activation(
                        xnTf[:, ft * TOKF + hh * FH + chi * wch:
                             ft * TOKF + hh * FH + chi * wch + wch],
                        pb[k % 2][:, 0:wch], AF.Copy)
                ev_xh_evac[gk] = OP("scalar", p4_h1e, waits=[ev_mm, ev_su])
                ev_h128mm_last = ev_mm
    ev_xh_done = ev_xh_evac[2 * FT * nch - 1]

    ev_ug_ld = {}
    ev_ug_mm = {}
    ev_ug_ev = {}
    ev_u = {}
    nld = 0
    for it in range(IT):
        for ki, wd in ((0, wupT_d), (1, wgT_d)):
            slot = nld % 3
            pn = nld - 3
            prev_mm = ev_ug_mm.get((pn // 2, pn % 2)) if pn >= 0 else None
            ev_ug_ld[(it, ki)] = OP("gpsimd",
                (lambda eng, wd=wd, it=it, slot=slot: eng.dma_start(
                    wring[slot].rearrange("p (t c) -> p t c", t=FT),
                    wd[:, it * 128:it * 128 + 128].rearrange(
                        "(t p) c -> p t c", p=128))),
                waits=[ev_attn_pe, prev_mm], dma="wu%d" % slot)
            nld += 1

            def p4_ug(eng, it=it, ki=ki, slot=slot):
                bk0 = (0 if it % 2 == 0 else 4) + 2 * ki
                for ft in range(FT):
                    for u in range(UPC):
                        ins = eng.matmul(
                            pb[bk0 + u][:, 0:UCH],
                            wring[slot][:, ft * 128:ft * 128 + 128],
                            xnTf[:, ft * TOKF + u * UCH:ft * TOKF + u * UCH + UCH],
                            start=(ft == 0), stop=(ft == FT - 1))
                return ins
            ev_ug_mm[(it, ki)] = OP("tensor", p4_ug,
                                    waits=[ev_ug_ld[(it, ki)], ev_xh_done,
                                           ev_ug_ev.get((it - 2, ki))])

            def p4_ugev(eng, it=it, ki=ki):
                bk0 = (0 if it % 2 == 0 else 4) + 2 * ki
                vec = upvec if ki == 0 else gatevec
                func = AF.Copy if ki == 0 else AF.Silu
                dstb = A_st if ki == 0 else B_st
                for u in range(UPC):
                    ins = eng.activation(dstb[:, u * UCH:(u + 1) * UCH],
                                         pb[bk0 + u][:, 0:UCH], func,
                                         scale=vec[:, it:it + 1])
                return ins
            ev_ug_ev[(it, ki)] = OP("scalar", p4_ugev,
                                    waits=[ev_ug_mm[(it, ki)],
                                           ev_u.get(it - 1)] + ev_sd_all)

        ev_u[it] = OP("vector", (lambda eng, it=it: eng.tensor_mul(
            Uap(it, 0, TOKF), A_st[:], B_st[:])),
            waits=[ev_ug_ev[(it, 0)], ev_ug_ev[(it, 1)], ev_attn_pe])
    ev_u_all = ev_u[IT - 1]

    ev_h2_mm = {}
    ev_h2_ev = {}
    for it in range(IT):
        for u in range(UPC):
            k = it * UPC + u
            if k >= 4:
                wprev = ev_h2_ev[((k - 4) // UPC, (k - 4) % UPC)]
            else:
                wprev = ev_ug_ev[(IT - 2, 1)] if IT >= 2 else None

            def p4_h2(eng, it=it, u=u, k=k):
                return eng.matmul(pb[k % 4][:, 0:UCH], h128_b[:],
                                  Uap(it, u * UCH, UCH), start=True, stop=True)
            ev_h2_mm[(it, u)] = OP("tensor", p4_h2,
                                   waits=[ev_u[it], ev_h128, wprev])

            def p4_h2e(eng, it=it, u=u, k=k):
                return eng.activation(Uap(it, u * UCH, UCH),
                                      pb[k % 4][:, 0:UCH], AF.Copy)
            ev_h2_ev[(it, u)] = OP("scalar", p4_h2e, waits=[ev_h2_mm[(it, u)]])
    ev_h2_last = ev_h2_ev[(IT - 1, UPC - 1)]

    ev_z = None
    ev_zH = None
    for sl in range(NSL):
        sl0 = sl * CS
        ename = "vector"
        cA, cB = C1, C2

        def p4_sA(eng, sl0=sl0, cA=cA):
            q = ITQ
            for half, (ua, ub) in enumerate(((Us[0], Us[2]), (Us[1], Us[3]))):
                a = ua.rearrange("p (t c) -> p t c", c=TOKF)[:, :, sl0:sl0 + CS]
                b = ub.rearrange("p (t c) -> p t c", c=TOKF)[:, :, sl0:sl0 + CS]
                c3 = cA.rearrange("p (t c) -> p t c", c=CS)
                eng.tensor_add(c3[:, half * q:(half + 1) * q, :], a, b)
                ins = eng.tensor_sub(
                    c3[:, 2 * q + half * q:2 * q + (half + 1) * q, :], a, b)
            return ins
        ev_sA = OP(ename, p4_sA,
                   waits=[ev_h2_last if sl == 0 else ev_z,
                          ev_ug_mm[(IT - 1, 1)] if sl == 0 else None])

        ev_stg = ev_sA
        src, dst = cA, cB
        for st in range(nst2 - 2, -1, -1):
            d = 1 << st
            last = (st == 0)

            def p4_s2(eng, src=src, dst=dst, d=d, last=last, sl0=sl0):
                if not last:
                    s4 = src.rearrange("p (g two w) -> p g two w", two=2,
                                       w=d * CS)
                    d4 = dst.rearrange("p (g two w) -> p g two w", two=2,
                                       w=d * CS)
                    eng.tensor_add(d4[:, :, 0, :], s4[:, :, 0, :],
                                   s4[:, :, 1, :])
                    return eng.tensor_sub(d4[:, :, 1, :], s4[:, :, 0, :],
                                          s4[:, :, 1, :])
                s4 = src.rearrange("p (g two w) -> p g two w", two=2, w=CS)
                for uq in range(4):
                    utv = Us[uq].rearrange("p (t two c) -> p t two c", two=2,
                                           c=TOKF)[:, :, :, sl0:sl0 + CS]
                    gg = uq * (ITQ // 2)
                    eng.tensor_add(utv[:, :, 0, :],
                                   s4[:, gg:gg + ITQ // 2, 0, :],
                                   s4[:, gg:gg + ITQ // 2, 1, :])
                    ins = eng.tensor_sub(utv[:, :, 1, :],
                                         s4[:, gg:gg + ITQ // 2, 0, :],
                                         s4[:, gg:gg + ITQ // 2, 1, :])
                return ins
            ev_stg = OP(ename, p4_s2, waits=[ev_stg])
            src, dst = dst, src
        ev_z = ev_stg
        if sl == NSL // 2 - 1:
            ev_zH = ev_z

    ev_wd_ld = {}
    ev_dn_mm = {}
    ev_gs = {}
    ev_comb = {}
    ev_outs = []
    gld = 0
    gc = 0
    NEOG = FT // 4
    FHT = FH
    for ps in range(2):
        tk0 = ps * FHT
        ev_zg = ev_zH if ps == 0 else ev_z
        for eog in range(NEOG):
            g2 = ps * NEOG + eog
            for it4 in range(IT // K2):
                slot = gld % NWDR
                prev_mm = ev_dn_mm.get(gld - NWDR)
                ev_wd_ld[(ps, eog, it4)] = OP("gpsimd",
                    (lambda eng, eog=eog, it4=it4, slot=slot: eng.dma_start(
                        wdring[slot].rearrange("p (b c) -> p b c", b=K2),
                        wdT_d[it4 * K2 * 128:(it4 + 1) * K2 * 128,
                              eog * 512:eog * 512 + 512].rearrange(
                                  "(b p) c -> p b c", p=128))),
                    waits=[ev_u_all, prev_mm], dma="wd%d" % slot)

                def p4_dn(eng, eog=eog, it4=it4, slot=slot, tk0=tk0, g2=g2):
                    bk0 = (g2 % 2) * 4
                    for iti in range(K2):
                        it = it4 * K2 + iti
                        for e4 in range(4):
                            ins = eng.matmul(
                                pb[bk0 + e4][:, 0:FHT],
                                wdring[slot][:, iti * 512 + e4 * 128:
                                             iti * 512 + e4 * 128 + 128],
                                Uap(it, tk0, FHT),
                                start=(it == 0), stop=(it == IT - 1))
                    return ins
                ws = [ev_wd_ld[(ps, eog, it4)]]
                if it4 == 0:
                    ws += [ev_zg]
                    ws += [ev_gs.get((g2 - 2, kkk)) for kkk in range(4)]
                ev_dn_mm[gld] = OP("tensor", p4_dn, waits=ws)
                gld += 1

            for e4 in range(4):
                eo = eog * 4 + e4
                par = gc % 2
                bk = (g2 % 2) * 4 + e4
                prev2 = ev_comb.get(gc - 2)
                prev_out = ev_outs[gc - 1] if gc >= 2 else None

                ev_g = OP("scalar", (lambda eng, bk=bk, eo=eo, par=par:
                    eng.activation(g_st[:, par * FHT:(par + 1) * FHT],
                                   pb[bk][:, 0:FHT], AF.Silu,
                                   scale=dnvec[:, eo:eo + 1])),
                    waits=[ev_dn_mm[gld - 1], prev2])
                ev_gs[(g2, e4)] = ev_g

                ev_ss = OP("scalar", (lambda eng, eo=eo, tk0=tk0, par=par:
                    eng.dma_start(ss_st[:, par * FHT:(par + 1) * FHT],
                                  s_dram[eo * 128:(eo + 1) * 128,
                                         tk0:tk0 + FHT])),
                    waits=ev_sd_all + [prev2, ev_zg], dma="st%d" % par)

                ev_xf = OP("scalar", (lambda eng, eo=eo, tk0=tk0, par=par:
                    eng.dma_start(xf_st[:, par * FHT:(par + 1) * FHT],
                                  xfT_d[eo * 128:(eo + 1) * 128,
                                        tk0:tk0 + FHT])),
                    waits=[prev2, ev_zg], dma="st%d" % par)

                def p4_comb(eng, par=par):
                    sch.chain("vector", eng, eng.tensor_mul(
                        o_st[:, par * FHT:(par + 1) * FHT],
                        g_st[:, par * FHT:(par + 1) * FHT],
                        ss_st[:, par * FHT:(par + 1) * FHT]))
                    return eng.tensor_add(
                        res_st[:, par * FHT:(par + 1) * FHT],
                        o_st[:, par * FHT:(par + 1) * FHT],
                        xf_st[:, par * FHT:(par + 1) * FHT])
                ev_cb = OP("vector", p4_comb,
                           waits=[ev_g, ev_ss, ev_xf, prev_out])
                ev_comb[gc] = ev_cb

                ev_outs.append(OP("sync", (lambda eng, eo=eo, tk0=tk0,
                                           par=par:
                    eng.dma_start(out_d[eo * 128:(eo + 1) * 128,
                                        tk0:tk0 + FHT],
                                  res_st[:, par * FHT:(par + 1) * FHT])),
                    waits=[ev_cb], dma="outd"))
                gc += 1

    n_out = sch.cnt["outd"]
    sch.ops["sync"].append(
        ((lambda eng: eng.nop()), (("outd", n_out, 16),), None))

    blk = nc.Block()
    blk.__enter__()
    sch.emit(blk)
    blk.__exit__(None, None, None)
    return nc


def _host_in_maps(cfg, inputs):
    x = np.ascontiguousarray(inputs["x"], dtype=np.float32)
    h128 = build_h128()
    slopes_all = np.asarray(cfg.slopes, np.float32)
    wqT = np.ascontiguousarray(inputs["Wq"].T)
    wkT = np.ascontiguousarray(inputs["Wk"].T)
    wvT = np.ascontiguousarray(inputs["Wv"].T)
    woT = np.ascontiguousarray(inputs["Wo"].T)
    wupT = np.ascontiguousarray(inputs["W_up"].T)
    wgT = np.ascontiguousarray(inputs["W_gate"].T)
    wdT = np.ascontiguousarray(inputs["W_down"].T)

    in_maps = []
    for c in range(8):
        b, r = c // 4, c % 4
        grps = [cfg.grp(j, r) for j in range(cfg.NSLOT)]
        cols = np.concatenate([np.arange(g * cfg.QG, (g + 1) * cfg.QG)
                               for g in grps])
        xb = x[b]
        in_maps.append({
            "xT": np.ascontiguousarray(xb.T),
            "xfT": np.ascontiguousarray(xb.T[:, cols]),
            "wqT": wqT, "wkT": wkT, "wvT": wvT, "woT": woT,
            "bq": np.ascontiguousarray(inputs["bq"]),
            "bk": np.ascontiguousarray(inputs["bk"]),
            "bv": np.ascontiguousarray(inputs["bv"]),
            "bo": np.ascontiguousarray(inputs["bo"]),
            "slopes": slopes_all,
            "mwin": build_mwin(cfg, r),
            "su_ug": np.ascontiguousarray(inputs["su_ug"]),
            "su_dp": np.ascontiguousarray(inputs["su_dp"]),
            "wsc": np.ascontiguousarray(inputs["wsc"]),
            "wdp": np.ascontiguousarray(inputs["wdp"]),
            "wupT": wupT, "wgT": wgT, "wdT": wdT,
            "h128": h128,
        })
    return in_maps


def run(cfg, inputs):
    nc = build_program(cfg)
    in_maps = _host_in_maps(cfg, inputs)
    res = run_bass_kernel_spmd(nc, in_maps, core_ids=list(range(8)))
    out = np.empty((2, cfg.S, cfg.E), np.float32)
    for c in range(8):
        b, r = c // 4, c % 4
        o = np.asarray(res.results[c]["out"])
        for j in range(cfg.NSLOT):
            g = cfg.grp(j, r)
            out[b, g * cfg.QG:(g + 1) * cfg.QG, :] = \
                o[:, j * cfg.QG:(j + 1) * cfg.QG].T
    return out


def kernel(L0_attn_x, L0_attn_Wq, L0_attn_bq, L0_attn_Wk, L0_attn_bk,
           L0_attn_Wv, L0_attn_bv, L0_attn_Wo, L0_attn_bo,
           L0_ffn_SU_ug, L0_ffn_SU_dp, L0_ffn_Wscale_ug, L0_ffn_Wscale_dp,
           L0_ffn_W_up, L0_ffn_W_gate, L0_ffn_W_down):
    cfg = Cfg(S=4096, E=2048, I=8192)
    inputs = dict(
        x=np.asarray(L0_attn_x, np.float32),
        Wq=np.asarray(L0_attn_Wq, np.float32),
        Wk=np.asarray(L0_attn_Wk, np.float32),
        Wv=np.asarray(L0_attn_Wv, np.float32),
        Wo=np.asarray(L0_attn_Wo, np.float32),
        bq=np.asarray(L0_attn_bq, np.float32),
        bk=np.asarray(L0_attn_bk, np.float32),
        bv=np.asarray(L0_attn_bv, np.float32),
        bo=np.asarray(L0_attn_bo, np.float32),
        su_ug=np.asarray(L0_ffn_SU_ug, np.float32),
        su_dp=np.asarray(L0_ffn_SU_dp, np.float32),
        wsc=np.asarray(L0_ffn_Wscale_ug, np.float32),
        wdp=np.asarray(L0_ffn_Wscale_dp, np.float32),
        W_up=np.asarray(L0_ffn_W_up, np.float32),
        W_gate=np.asarray(L0_ffn_W_gate, np.float32),
        W_down=np.asarray(L0_ffn_W_down, np.float32),
    )
    return run(cfg, inputs)


# version 18
# speedup vs baseline: 1.0053x; 1.0053x over previous
import math
import numpy as np
import concourse.bass as bass
import concourse.mybir as mybir
from concourse.bass_utils import run_bass_kernel_spmd

F32 = mybir.dt.float32
F16 = mybir.dt.float16
BF16 = mybir.dt.bfloat16
AF = mybir.ActivationFunctionType
ALU = mybir.AluOpType

NEG_BIG = -60000.0
BAND_TH = 100.0


def alibi_slopes(n):
    def pow2_slopes(n):
        start = 2 ** (-(2 ** -(math.log2(n) - 3)))
        return [start * start ** i for i in range(n)]
    if math.log2(n).is_integer():
        return pow2_slopes(n)
    c = 2 ** math.floor(math.log2(n))
    return pow2_slopes(c) + alibi_slopes(2 * c)[::2][: n - c]


class Cfg:
    def __init__(self, S=4096, E=2048, I=8192):
        self.S, self.E, self.I = S, E, I
        self.QG = 256
        self.H = E // 128
        self.FT = E // 128
        self.IT = I // 128
        self.TOKF = S // 4
        self.NGRP = S // self.QG
        self.NSLOT = self.NGRP // 4
        self.SC = 1.0 / math.sqrt(128.0)
        self.CS = 32
        self.slopes = alibi_slopes(self.H)
        self.caps = [(4 * j + 4) * (self.QG // 128)
                     for j in range(self.NSLOT)]
        self.band = [int(math.ceil((BAND_TH / s + self.QG) / 128.0)) + 6
                     for s in self.slopes]
        self.W = [[min(self.caps[j], self.band[h]) for h in range(self.H)]
                  for j in range(self.NSLOT)]
        self.mw_w = [self.caps[j] * 128 + self.QG - 128
                     for j in range(self.NSLOT)]

    def grp(self, j, r):
        return 4 * j + (r if j % 2 == 0 else 3 - r)


def build_mwin(cfg, r):
    cols = []
    for j in range(cfg.NSLOT):
        g0 = cfg.grp(j, r) * cfg.QG
        w = np.arange(cfg.mw_w[j])[None, :]
        p = np.arange(128)[:, None]
        dist = g0 + w - (cfg.caps[j] - 1) * 128 - p
        cols.append(np.where(dist >= 0, -dist, NEG_BIG))
    return np.ascontiguousarray(np.concatenate(cols, axis=1).astype(np.float32))


def build_h128():
    h = np.array([[1.0]], dtype=np.float32)
    for _ in range(7):
        h = np.block([[h, h], [h, -h]])
    return h.astype(np.float32)


class Sched:
    ENGINES = ("sync", "gpsimd", "tensor", "vector", "scalar")

    def __init__(self, nc):
        self.nc = nc
        self.ops = {e: [] for e in self.ENGINES}
        self.sems = {}
        self.cnt = {}
        self.esem = {"tensor": "pe", "vector": "dve", "scalar": "act",
                     "gpsimd": "pool", "sync": "sp"}

    def sem(self, name):
        if name not in self.sems:
            self.sems[name] = self.nc.alloc_semaphore("s_" + name)
            self.cnt[name] = 0
        return self.sems[name]

    def op(self, eng, fn, waits=(), dma=None, inc=True):
        ev = None
        if dma is not None:
            self.sem(dma)
            self.cnt[dma] += 16
            ev = (dma, self.cnt[dma], 16)
        elif inc:
            s = self.esem[eng]
            self.sem(s)
            self.cnt[s] += 1
            ev = (s, self.cnt[s], 1)
        ws = tuple(w for w in waits if w is not None)
        self.ops[eng].append((fn, ws, ev))
        return ev

    def chain(self, eng_name, E, ins):
        s = self.sem(self.esem[eng_name] + "ch")
        self.cnt[self.esem[eng_name] + "ch"] += 1
        v = self.cnt[self.esem[eng_name] + "ch"]
        ins.then_inc(self.sems[self.esem[eng_name] + "ch"], 1)
        E.wait_ge(self.sems[self.esem[eng_name] + "ch"], v)

    def emit(self, blk):
        for eng in self.ENGINES:
            lst = self.ops[eng]
            if not lst:
                continue

            def body(E, lst=lst):
                done = {}
                for fn, ws, ev in lst:
                    need = {}
                    for (s, v, _b) in ws:
                        if v > need.get(s, 0):
                            need[s] = v
                    for s, v in need.items():
                        if v > done.get(s, 0):
                            E.wait_ge(self.sems[s], v)
                            done[s] = v
                    ins = fn(E)
                    if ev is not None:
                        ins.then_inc(self.sems[ev[0]], ev[2])
            getattr(blk, eng)(body)


def build_program(cfg):
    S, E, I, QG = cfg.S, cfg.E, cfg.I, cfg.QG
    FT, IT, H = cfg.FT, cfg.IT, cfg.H
    NSLOT, CS = cfg.NSLOT, cfg.CS
    TOKF = cfg.TOKF
    NSLAB_A = S // 128
    NSLAB_F = TOKF // 128
    NSLAB = NSLAB_A + NSLAB_F
    FH = TOKF // 2
    ITQ = IT // 4
    THN = max(1, TOKF // 512)
    TCH = TOKF // THN
    UPC = max(1, TOKF // 512)
    UCH = TOKF // UPC
    nst1 = int(math.log2(FT))
    nst2 = int(math.log2(IT))
    NSL = TOKF // CS
    KT = S // 128
    VCH = E // 512
    QCH = TOKF // 512 if TOKF >= 512 else 1
    QW = min(512, TOKF)
    MWTOT = sum(cfg.mw_w)

    nc = bass.Bass("TRN2", target_bir_lowering=False)
    din = lambda n, sh: nc.dram_tensor(n, sh, F32, kind="ExternalInput").ap()
    xT_d = din("xT", [E, S])
    xfT_d = din("xfT", [E, TOKF])
    wqT_d = din("wqT", [E, E])
    wkT_d = din("wkT", [E, E])
    wvT_d = din("wvT", [E, E])
    woT_d = din("woT", [E, E])
    bq_d = din("bq", [E])
    bk_d = din("bk", [E])
    bv_d = din("bv", [E])
    bo_d = din("bo", [E])
    slopes_d = din("slopes", [H])
    mwin_d = din("mwin", [128, MWTOT])
    su_ug_d = din("su_ug", [E])
    su_dp_d = din("su_dp", [I])
    wsc_d = din("wsc", [2 * I])
    wdp_d = din("wdp", [E])
    wupT_d = din("wupT", [E, I])
    wgT_d = din("wgT", [E, I])
    wdT_d = din("wdT", [I, E])
    h128_d = din("h128", [128, 128])
    out_d = nc.dram_tensor("out", [E, TOKF], F32, kind="ExternalOutput").ap()
    k_dram = nc.dram_tensor("k_scr", [H, 128, S], BF16, kind="Internal").ap()
    v_dram = nc.dram_tensor("v_scr", [S, E], BF16, kind="Internal").ap()
    s_dram = nc.dram_tensor("s_scr", [E, TOKF], BF16, kind="Internal").ap()

    BASE0 = ((int(nc.sbuf_base) + 63) // 64) * 64

    def at(name, shape, dt, o):
        return nc.alloc_sbuf_tensor_at(name, shape, dt, offset=BASE0 + o).ap()

    _o = [0]
    def cst(name, shape, dt):
        o = (_o[0] + 63) // 64 * 64
        ap = at(name, shape, dt, o)
        _o[0] = o + int(np.prod(shape[1:])) * mybir.dt.size(dt)
        return ap

    ones1_b = cst("ones1_b", [1, 128], BF16)
    ones128_b = cst("ones128_b", [128, 1], BF16)
    ones1_f = cst("ones1_f", [1, 128], F32)
    h128_b = cst("h128_b", [128, 128], BF16)
    su_ug_l = cst("su_ug_l", [128, FT], F32)
    su_dp_l = cst("su_dp_l", [128, IT], F32)
    upvec = cst("upvec", [128, IT], F32)
    gatevec = cst("gatevec", [128, IT], F32)
    dnvec = cst("dnvec", [128, FT], F32)
    bq_l = cst("bq_l", [128, H], F32)
    bk_l = cst("bk_l", [128, H], F32)
    bo_l = cst("bo_l", [128, FT], F32)
    slopes_r = cst("slopes_r", [1, H], F32)
    slopes_l = cst("slopes_l", [128, H], F32)
    bvB = cst("bvB", [128, E], BF16)
    m_row = cst("m_row", [1, 256], F32)
    sq_row = cst("sq_row", [1, 256], F32)
    msq_row = cst("msq_row", [1, 256], F32)
    rstd_row = cst("rstd_row", [1, 256], F32)
    var_row = cst("var_row", [1, 256], F32)
    mrstd_row = cst("mrstd_row", [1, 256], F32)
    rstdB = cst("rstdB", [128, 256], F32)
    mrstdB = cst("mrstdB", [128, 256], F32)
    xn_scr = cst("xn_scr", [128, 128], F32)
    kstg = [cst(f"kstg{i}", [128, H * 128], BF16) for i in range(2)]
    vstg = [cst(f"vstg{i}", [128, 512], BF16) for i in range(2)]
    sstg = [cst(f"sstg{i}", [128, QG], BF16) for i in range(2)]
    R0 = ((_o[0] + 63) // 64) * 64

    sz_U4 = ITQ * TOKF * 2
    szA = max(FT * E * 2, MWTOT * 2 + 4 * S * 2 + 4 * QG * 2 + 4 * QG * 4,
              2 * FT * FH * 4, 2 * sz_U4)
    szB = max(FT * E * 2, H * TOKF * 2 + H * NSLOT * QG * 2, 2 * sz_U4)
    e_stage = 2 * IT * CS * 4 + 4 * FH * 2 + 6 * FH * 4
    szC = max(FT * TOKF * 2, e_stage)
    szD = max(5 * FT * 256,
              E * 4,
              3 * FT * 256 + 2 * TOKF * 4,
              3 * 1024)
    A = R0
    B = A + szA
    Cr = B + szB
    D = Cr + szC
    TOP = D + szD
    LIMIT = 229300 - BASE0 - 1024
    assert TOP <= LIMIT, (
        f"sbuf over: {TOP} > {LIMIT} (A {szA} B {szB} C {szC} D {szD} R0 {R0})")

    wk_b = at("wk_b", [128, FT * E], BF16, A)
    wq_b = at("wq_b", [128, FT * E], BF16, A)
    wo_b = at("wo_b", [128, FT * E], BF16, A)
    mwin_s = at("mwin_s", [128, MWTOT], F16, A)
    kh = [at(f"kh{i}", [128, S], BF16, A + MWTOT * 2 + i * S * 2)
          for i in range(2)]
    vh = [at(f"vh{i}", [128, KT * 128], BF16, A + MWTOT * 2 + (2 + i) * S * 2)
          for i in range(2)]
    ETs = [at(f"ET{i}", [128, QG], BF16, A + MWTOT * 2 + 4 * S * 2 + i * QG * 2)
           for i in range(4)]
    _pA = A + MWTOT * 2 + 4 * S * 2 + 4 * QG * 2
    rinv2 = at("rinv2", [1, 2 * QG], F32, _pA)
    rbB2 = at("rbB2", [128, 2 * QG], F32, _pA + 2 * QG * 4)
    assert _pA + 2 * QG * 4 + 2 * QG * 4 <= A + szA
    Fbuf = at("Fbuf", [128, FT * FH], F32, A)
    F2buf = at("F2buf", [128, FT * FH], F32, A + FT * FH * 4)
    F2b = at("F2b", [128, FT * FH], BF16,
             A if nst1 % 2 == 0 else A + FT * FH * 4)
    wv_b = at("wv_b", [128, FT * E], BF16, B)
    q_sb = at("q_sb", [128, H * TOKF], BF16, B)
    OT_sb = at("OT_sb", [128, H * NSLOT * QG], BF16, B + H * TOKF * 2)
    Us = [at("U0", [128, ITQ * TOKF], BF16, A),
          at("U1", [128, ITQ * TOKF], BF16, A + sz_U4),
          at("U2", [128, ITQ * TOKF], BF16, B),
          at("U3", [128, ITQ * TOKF], BF16, B + sz_U4)]
    xnTf = at("xnTf", [128, FT * TOKF], BF16, Cr)
    C1 = at("C1", [128, IT * CS], F32, Cr)
    C2 = at("C2", [128, IT * CS], F32, Cr + IT * CS * 4)
    p4c = Cr + 2 * IT * CS * 4
    g_st = at("g_st", [128, 2 * FH], BF16, p4c)
    ss_st = at("ss_st", [128, 2 * FH], BF16, p4c + FH * 4)
    xf_st = at("xf_st", [128, 2 * FH], F32, p4c + FH * 8)
    o_st = at("o_st", [128, 2 * FH], F32, p4c + FH * 16)
    res_st = at("res_st", [128, 2 * FH], F32, p4c + FH * 24)
    xTb = [at(f"xTb{i}", [128, FT * 128], BF16, D + i * FT * 256)
           for i in range(2)]
    xnT2 = [at(f"xnT2{i}", [128, FT * 128], BF16, D + (2 + i) * FT * 256)
            for i in range(2)]
    sq_scr = at("sq_scr", [128, FT * 128], BF16, D + 4 * FT * 256)
    bv_r = at("bv_r", [1, E], F32, D)
    wring = [at(f"wring{i}", [128, FT * 128], BF16, D + i * FT * 256)
             for i in range(3)]
    A_st = at("A_st", [128, TOKF], F32, D + 3 * FT * 256)
    B_st = at("B_st", [128, TOKF], F32, D + 3 * FT * 256 + TOKF * 4)
    K2 = next(k for k in (4, 2, 1)
              if IT % k == 0 and 4 * k * 1024 <= szD)
    NWDR = 5 if 5 * K2 * 1024 <= szD else 4
    wdring = [at(f"wdring{i}", [128, K2 * 512], BF16, D + i * K2 * 1024)
              for i in range(NWDR)]
    assert NWDR * K2 * 1024 <= szD

    def Uap(it, c0, w):
        t = Us[it // ITQ]
        base = (it % ITQ) * TOKF
        return t[:, base + c0: base + c0 + w]

    pb = [nc.alloc_psum_tensor(f"pb{i}", [128, 512], F32).ap() for i in range(8)]
    nc.sbuf_base = BASE0 + ((TOP + 63) // 64) * 64

    sch = Sched(nc)
    OP = sch.op

    c0_evs = []
    def mk_cdma(dst, src):
        return lambda eng, dst=dst, src=src: eng.dma_start(
            dst[:], src, allow_slow_non_contiguous=True)
    for dst, src in [
        (su_ug_l, su_ug_d.rearrange("(t p) -> p t", p=128)),
        (su_dp_l, su_dp_d.rearrange("(t p) -> p t", p=128)),
        (upvec, wsc_d[0:I].rearrange("(t p) -> p t", p=128)),
        (gatevec, wsc_d[I:2 * I].rearrange("(t p) -> p t", p=128)),
        (dnvec, wdp_d.rearrange("(t p) -> p t", p=128)),
        (bq_l, bq_d.rearrange("(t p) -> p t", p=128)),
        (bk_l, bk_d.rearrange("(t p) -> p t", p=128)),
        (bo_l, bo_d.rearrange("(t p) -> p t", p=128)),
        (slopes_r, slopes_d[None, :]),
        (bv_r, bv_d[None, :]),
    ]:
        c0_evs.append(OP("sync", mk_cdma(dst, src), dma="c0"))
    ev_c0 = c0_evs[-1]

    ev_h128 = OP("gpsimd", lambda eng: eng.dma_start(h128_b[:], h128_d[:]),
                 dma="cg")
    OP("gpsimd", lambda eng: eng.memset(ones1_b[:], 1.0), inc=False)
    OP("gpsimd", lambda eng: eng.memset(ones128_b[:], 1.0), inc=False)
    ev_ones = OP("gpsimd", lambda eng: eng.memset(ones1_f[:], 1.0))
    ev_wk = OP("gpsimd", lambda eng: eng.dma_start(
        wk_b.rearrange("p (t d) -> p t d", t=FT),
        wkT_d.rearrange("(t p) d -> p t d", p=128)), dma="wkd")
    ev_wv = OP("gpsimd", lambda eng: eng.dma_start(
        wv_b.rearrange("p (t d) -> p t d", t=FT),
        wvT_d.rearrange("(t p) d -> p t d", p=128)), dma="wvd")

    def p0_vec(eng):
        inv1 = 1.0 / math.sqrt(E)
        sch.chain("vector", eng,
                  eng.tensor_scalar(upvec[:], upvec[:], inv1, None, ALU.mult))
        sch.chain("vector", eng,
                  eng.tensor_mul(upvec[:], upvec[:], su_dp_l[:]))
        eng.tensor_scalar(gatevec[:], gatevec[:], inv1, None, ALU.mult)
        eng.tensor_scalar(dnvec[:], dnvec[:], 1.0 / math.sqrt(I), None,
                          ALU.mult)
        return eng.tensor_scalar(bq_l[:], bq_l[:], cfg.SC, None, ALU.mult)
    ev_vec0 = OP("vector", p0_vec, waits=c0_evs)

    ev_pe0 = OP("tensor", (lambda eng: eng.matmul(
        pb[6][:, 0:H], ones1_f[:], slopes_r[:], start=True, stop=True)),
        waits=[ev_c0, ev_ones] + c0_evs[-2:])
    ev_slo = OP("scalar", (lambda eng: eng.copy(slopes_l[:], pb[6][:, 0:H])),
                waits=[ev_pe0])

    ev_bvB = None
    for ch in range(E // 512):
        ev_m = OP("tensor", (lambda eng, ch=ch: eng.matmul(
            pb[7][:, 0:512], ones1_f[:], bv_r[0:1, ch * 512:(ch + 1) * 512],
            start=True, stop=True)), waits=[ev_pe0, ev_bvB])
        ev_bvB = OP("scalar", (lambda eng, ch=ch: eng.copy(
            bvB[:, ch * 512:(ch + 1) * 512], pb[7][:, 0:512])), waits=[ev_m])

    ev_ld = {}
    ev_bcE = {}
    ev_sq = {}
    ev_sqmm = {}
    ev_rows = {}
    ev_rstd = {}
    ev_bcast_ev = {}
    ev_cons_all = {}
    ev_xnt = {}
    ev_kdma = {}
    ev_vdma = {}
    ev_kevac = {}
    ev_vevac = {}

    emit_order = []

    def ln_head(slab, li):
        is_ffn = slab >= NSLAB_A
        cc0 = (slab - NSLAB_A) * 128 if is_ffn else slab * 128
        src_t = xfT_d if is_ffn else xT_d
        par = li % 2
        psem = "xt%d" % par
        prev = emit_order[li - 1] if li >= 1 else None
        pprev = emit_order[li - 2] if li >= 2 else None

        ev_ld[slab] = OP("gpsimd", (lambda eng, s=src_t, c=cc0, p=par:
            eng.dma_start(xTb[p].rearrange("p (t c) -> p t c", t=FT),
                          s[:, c:c + 128].rearrange("(t p) c -> p t c",
                                                    p=128))),
            waits=[ev_xnt.get(pprev),
                   ev_bvB if li <= 1 else None], dma=psem)

        ev_sq[slab] = OP("vector", (lambda eng, par=par: eng.tensor_mul(
            sq_scr[:], xTb[par][:], xTb[par][:])),
            waits=[ev_ld[slab], ev_sqmm.get(prev)])

        def p1_stats(eng, par=par, ev_s=ev_sq[slab]):
            for ft in range(FT):
                eng.matmul(pb[0][0:1, 0:128], ones128_b[:],
                           xTb[par][:, ft * 128:ft * 128 + 128],
                           start=(ft == 0), stop=(ft == FT - 1))
            eng.wait_ge(sch.sems[ev_s[0]], ev_s[1])
            for ft in range(FT):
                ins = eng.matmul(pb[0][0:1, 128:256], ones128_b[:],
                                 sq_scr[:, ft * 128:ft * 128 + 128],
                                 start=(ft == 0), stop=(ft == FT - 1))
            return ins
        ev_sqmm[slab] = OP("tensor", p1_stats,
                           waits=[ev_ld[slab], ev_rows.get(prev)])

        pr = lambda t, q=par: t[0:1, q * 128:q * 128 + 128]

        def p1_rows(eng, par=par):
            eng.activation(pr(m_row), pb[0][0:1, 0:128], AF.Copy,
                           scale=1.0 / E)
            return eng.activation(pr(sq_row), pb[0][0:1, 128:256], AF.Copy,
                                  scale=1.0 / E)
        ev_rows[slab] = OP("scalar", p1_rows,
                           waits=[ev_sqmm[slab], ev_rstd.get(pprev)])

        def p1_var(eng, par=par):
            sch.chain("vector", eng,
                      eng.tensor_mul(pr(msq_row), pr(m_row), pr(m_row)))
            return eng.scalar_tensor_tensor(pr(var_row), pr(sq_row), 1e-5,
                                            pr(msq_row), ALU.add,
                                            ALU.subtract)
        ev_var = OP("vector", p1_var, waits=[ev_rows[slab]])

        ev_sd = OP("scalar", (lambda eng, par=par: eng.activation(
            pr(msq_row), pr(var_row), AF.Sqrt)), waits=[ev_var])

        def p1_rstd(eng, par=par):
            sch.chain("vector", eng,
                      eng.reciprocal(pr(rstd_row), pr(msq_row)))
            return eng.tensor_mul(pr(mrstd_row), pr(m_row), pr(rstd_row))
        ev_rstd[slab] = OP("vector", p1_rstd,
                           waits=[ev_sd, ev_bcE.get(pprev)])

    def ln_tail(slab, li):
        is_ffn = slab >= NSLAB_A
        cc0 = (slab - NSLAB_A) * 128 if is_ffn else slab * 128
        par = li % 2
        prev = emit_order[li - 1] if li >= 1 else None
        pprev = emit_order[li - 2] if li >= 2 else None
        pr = lambda t, q=par: t[0:1, q * 128:q * 128 + 128]

        def p1_bc(eng, par=par):
            eng.matmul(pb[1][:, 0:128], ones1_f[:], pr(rstd_row), start=True,
                       stop=True)
            return eng.matmul(pb[1][:, 128:256], ones1_f[:], pr(mrstd_row),
                              start=True, stop=True)
        ev_bc = OP("tensor", p1_bc,
                   waits=[ev_rstd[slab], ev_bcast_ev.get(prev), ev_ones])
        ev_bcE[slab] = ev_bc

        def p1_bcev(eng, par=par):
            eng.copy(rstdB[:, par * 128:par * 128 + 128], pb[1][:, 0:128])
            return eng.copy(mrstdB[:, par * 128:par * 128 + 128],
                            pb[1][:, 128:256])
        ev_bcast_ev[slab] = OP("scalar", p1_bcev,
                               waits=[ev_bc, ev_xnt.get(pprev)])

        def p1_xnt(eng, par=par, is_ffn=is_ffn, cc0=cc0):
            for ft in range(FT):
                sch.chain("vector", eng, eng.tensor_mul(
                    xn_scr[:], xTb[par][:, ft * 128:ft * 128 + 128],
                    rstdB[:, par * 128:par * 128 + 128]))
                dst = (xnTf[:, ft * TOKF + cc0: ft * TOKF + cc0 + 128]
                       if is_ffn else xnT2[par][:, ft * 128:ft * 128 + 128])
                ins = eng.tensor_sub(dst, xn_scr[:],
                                     mrstdB[:, par * 128:par * 128 + 128])
                if ft < FT - 1:
                    sch.chain("vector", eng, ins)
            return ins
        ev_xnt[slab] = OP("vector", p1_xnt,
                          waits=[ev_bcast_ev[slab], ev_vec0,
                                 ev_xnt.get(prev),
                                 ev_cons_all.get(slab - 2)
                                 if not is_ffn else None])
        if is_ffn:
            ev_cons_all[slab] = ev_sqmm[slab]

    def emit_kv(slab, par2, tail_cb):
        par = slab % 2
        ev_kc = {}
        for hch in range(H):
            def p1_k(eng, hch=hch, par2=par2):
                bank = pb[2 + (hch % 2)]
                for ft in range(FT):
                    ins = eng.matmul(
                        bank[:, 0:128],
                        wk_b[:, (ft * H + hch) * 128:(ft * H + hch) * 128 + 128],
                        xnT2[par2][:, ft * 128:ft * 128 + 128],
                        start=(ft == 0), stop=(ft == FT - 1))
                return ins
            wprev = (ev_kevac.get((slab, hch - 2)) if hch >= 2
                     else ev_kevac.get((slab - 1, H - 2 + (hch % 2))))
            ev_kc[hch] = OP("tensor", p1_k,
                            waits=[ev_xnt[slab], ev_wk, wprev])
            if hch == 1:
                tail_cb()

            def p1_kev(eng, hch=hch, par=par):
                bank = pb[2 + (hch % 2)]
                return eng.activation(
                    kstg[par][:, hch * 128:hch * 128 + 128],
                    bank[:, 0:128], AF.Identity,
                    bias=bk_l[:, hch:hch + 1])
            ev_kevac[(slab, hch)] = OP(
                "scalar", p1_kev,
                waits=[ev_kc[hch],
                       ev_kdma.get(slab - 2) if hch == 0 else None])

        def p1_kdma(eng, slab=slab, par=par):
            dst = k_dram[:, :, slab * 128:slab * 128 + 128].rearrange(
                "h d t -> d h t")
            return eng.dma_start(dst,
                                 kstg[par].rearrange("p (h t) -> p h t",
                                                     h=H))
        ev_kdma[slab] = OP("scalar", p1_kdma,
                           waits=[ev_kevac[(slab, H - 1)]],
                           dma="kd%d" % par)

        ev_vc = {}
        for vch in range(VCH):
            def p1_v(eng, vch=vch, par2=par2):
                bank = pb[4 + (vch % 2)]
                for ft in range(FT):
                    ins = eng.matmul(
                        bank[:, 0:512],
                        xnT2[par2][:, ft * 128:ft * 128 + 128],
                        wv_b[:, ft * E + vch * 512:ft * E + vch * 512 + 512],
                        start=(ft == 0), stop=(ft == FT - 1))
                return ins
            wprev = (ev_vevac.get((slab, vch - 2)) if vch >= 2
                     else ev_vevac.get((slab - 1, VCH - 2 + (vch % 2))))
            ev_vc[vch] = OP("tensor", p1_v,
                            waits=[ev_xnt[slab], ev_wv, wprev])

            def p1_vev(eng, vch=vch):
                bank = pb[4 + (vch % 2)]
                return eng.tensor_add(vstg[vch % 2][:], bank[:, 0:512],
                                      bvB[:, vch * 512:vch * 512 + 512])
            ev_vevac[(slab, vch)] = OP(
                "vector", p1_vev,
                waits=[ev_vc[vch], ev_bvB,
                       ev_vdma.get((slab, vch - 2)),
                       ev_vdma.get((slab - 1, VCH - 2 + (vch % 2)))])

            def p1_vdma(eng, slab=slab, vch=vch):
                dst = v_dram[slab * 128:slab * 128 + 128,
                             vch * 512:vch * 512 + 512]
                return eng.dma_start(dst, vstg[vch % 2][:])
            ev_vdma[(slab, vch)] = OP("scalar", p1_vdma,
                                      waits=[ev_vevac[(slab, vch)]],
                                      dma="vd%d" % (vch % 2))
        ev_cons_all[slab] = ev_vc[VCH - 1]

    step = NSLAB_A // max(1, NSLAB_F // 2)
    fi = NSLAB_A
    for a in range(NSLAB_A):
        emit_order.append(a)
        if (a + 1) % step == 0 and fi < NSLAB:
            emit_order.append(fi)
            emit_order.append(fi + 1)
            fi += 2
    while fi < NSLAB:
        emit_order.append(fi)
        fi += 1
    pos_attn = {s: p for p, s in enumerate(emit_order) if s < NSLAB_A}
    pos_of = {s: p for p, s in enumerate(emit_order)}

    tailq = []
    li = [0]

    def pop_tail():
        s, lpos = tailq.pop(0)
        ln_tail(s, lpos)

    def pump(limit):
        while li[0] < NSLAB and li[0] <= limit:
            while tailq and tailq[0][1] <= li[0] - 2:
                pop_tail()
            s = emit_order[li[0]]
            ln_head(s, li[0])
            tailq.append((s, li[0]))
            li[0] += 1

    pump(1)
    pop_tail()
    for ki in range(NSLAB_A):
        pump(min(NSLAB - 1, pos_attn[ki] + 2))

        def tail_cb(ki=ki):
            while tailq and (tailq[0][0] >= NSLAB_A
                             or tailq[0][0] <= ki + 1):
                pop_tail()
        emit_kv(ki, pos_attn[ki] % 2, tail_cb)
    pump(NSLAB - 1)
    while tailq:
        pop_tail()

    ev_p1_pe = ev_cons_all[NSLAB - 1]
    ev_spills = [(s, sch.cnt[s], 16) for s in ("kd0", "kd1", "vd0", "vd1")]

    ev_wq = OP("gpsimd", lambda eng: eng.dma_start(
        wq_b.rearrange("p (t d) -> p t d", t=FT),
        wqT_d.rearrange("(t p) d -> p t d", p=128)),
        waits=[ev_cons_all[NSLAB_A - 1]], dma="wqd")

    ev_qc = {}
    ev_qevac = {}
    for hch in range(H):
        for u in range(QCH):
            k2 = hch * QCH + u

            def p2_q(eng, hch=hch, u=u, k2=k2):
                bank = pb[2 + (k2 % 2)]
                for ft in range(FT):
                    ins = eng.matmul(
                        bank[:, 0:QW],
                        wq_b[:, (ft * H + hch) * 128:(ft * H + hch) * 128 + 128],
                        xnTf[:, ft * TOKF + u * QW:ft * TOKF + u * QW + QW],
                        start=(ft == 0), stop=(ft == FT - 1))
                return ins
            xn_need = NSLAB_A + min(NSLAB_F, 4 * (u + 1)) - 1
            lastk = H - 2 + ((H - 2) % 2 != k2 % 2)
            ev_qc[k2] = OP("tensor", p2_q,
                           waits=[ev_wq, ev_xnt[xn_need],
                                  ev_qevac.get(k2 - 2),
                                  ev_kevac[(NSLAB_A - 1, lastk)]
                                  if k2 <= 1 else None])

            def p2_qe(eng, hch=hch, u=u, k2=k2):
                bank = pb[2 + (k2 % 2)]
                return eng.activation(
                    q_sb[:, hch * TOKF + u * QW:hch * TOKF + u * QW + QW],
                    bank[:, 0:QW], AF.Identity, bias=bq_l[:, hch:hch + 1],
                    scale=cfg.SC)
            ev_qevac[k2] = OP("scalar", p2_qe, waits=[ev_qc[k2]])
    ev_q_done = ev_qevac[H * QCH - 1]

    ev_mw = OP("gpsimd", lambda eng: eng.dma_start(mwin_s[:], mwin_d[:]),
               waits=[ev_q_done], dma="mwd")

    mw_base = []
    acc = 0
    for j in range(NSLOT):
        mw_base.append(acc)
        acc += cfg.mw_w[j]

    PV_LA = 3
    bank_last_exp = [None] * 4
    et_last_pv = [None] * 4
    ev_khl = {}
    ev_vhl = {}
    ev_head_done = {}
    ev_evac = {}
    ev_bcast = {}
    gidx = 0
    hj = 0
    pn = None

    def norm_recip(pn):
        _hj, pvl, pvb, par = pn
        return OP("vector", (lambda eng, par=par: eng.reciprocal(
            rinv2[0:1, par * QG:par * QG + QG], pb[6 + par][0:1, 0:QG])),
            waits=[pvl, ev_bcast.get(pn[0] - 2)])

    def norm_bcast(pn, ev_ri):
        _hj, pvl, pvb, par = pn
        ev_bcast[_hj] = OP("tensor", (lambda eng, par=par:
            eng.matmul(pb[6 + par][:, 0:QG], ones1_f[:],
                       rinv2[0:1, par * QG:par * QG + QG],
                       start=True, stop=True)), waits=[ev_ri])
        return ev_bcast[_hj]

    def norm_rbc(pn, ev_rb):
        _hj, pvl, pvb, par = pn
        return OP("scalar", (lambda eng, par=par: eng.copy(
            rbB2[:, par * QG:par * QG + QG], pb[6 + par][:, 0:QG])),
            waits=[ev_rb])

    def norm_evac(pn, ev_rc, oslot):
        _hj, pvl, pvb, par = pn
        ev_evac[_hj] = OP("vector", (lambda eng, pvb=pvb, par=par,
                                     oslot=oslot:
            eng.tensor_mul(OT_sb[:, oslot * QG:oslot * QG + QG],
                           pvb[:, 0:QG], rbB2[:, par * QG:par * QG + QG])),
            waits=[ev_rc])
        return ev_evac[_hj]

    oslot_of = {}
    for h in range(H):
        hp = h % 2
        ev_khl[h] = OP("sync", (lambda eng, h=h, hp=hp: eng.dma_start(
            kh[hp][:], k_dram[h, :, :])),
            waits=ev_spills + [ev_head_done.get(h - 2),
                   ev_mw if h <= 1 else None], dma="kh%d" % hp)
        ev_vhl[h] = OP("sync", (lambda eng, h=h, hp=hp: eng.dma_start(
            vh[hp].rearrange("p (t d) -> p t d", t=KT),
            v_dram[:, h * 128:h * 128 + 128].rearrange("(t p) d -> p t d",
                                                       p=128))),
            waits=ev_spills + [ev_head_done.get(h - 2),
                   ev_mw if h <= 1 else None], dma="vh%d" % hp)

        for j in range(NSLOT):
            cap = cfg.caps[j]
            W = cfg.W[j][h]
            assert W > PV_LA + 2
            q_ap = q_sb[:, h * TOKF + j * QG: h * TOKF + j * QG + QG]
            par = hj % 2
            pvb = pb[4 + par]
            oslot_of[hj] = h * NSLOT + j

            ev_st = {}
            ev_al = {}
            ev_ex = {}
            ev_pv = {}
            ev_ri_p = None
            g0 = gidx

            def emit_pv(idx, g0=g0, cap=cap, W=W, hp=hp, pvb=pvb, h=h):
                rho = cap - W + idx
                e4 = (g0 + idx) % 4

                def fn(eng, rho=rho, e4=e4, idx=idx, W=W, hp=hp, pvb=pvb,
                       par=par):
                    eng.matmul(pvb[:, 0:QG],
                               vh[hp][:, rho * 128:rho * 128 + 128],
                               ETs[e4][:], start=(idx == 0),
                               stop=(idx == W - 1))
                    return eng.matmul(pb[6 + par][0:1, 0:QG], ones128_b[:],
                                      ETs[e4][:], start=(idx == 0),
                                      stop=(idx == W - 1))
                ev_pv[idx] = OP("tensor", fn,
                                waits=[ev_ex[idx],
                                       ev_evac.get(hj - 2) if idx == 0
                                       else None,
                                       ev_vhl[h] if idx == 0 else None])
                et_last_pv[e4] = ev_pv[idx]

            for idx in range(W):
                rho = cap - W + idx
                b4 = gidx % 4
                e4 = gidx % 4

                def mk_st(rho=rho, b4=b4, q_ap=q_ap, hp=hp):
                    def fn(eng):
                        return eng.matmul(pb[b4][:, 0:QG],
                                          kh[hp][:, rho * 128:rho * 128 + 128],
                                          q_ap, start=True, stop=True)
                    return fn
                ev_st[idx] = OP("tensor", mk_st(),
                                waits=[bank_last_exp[b4],
                                       ev_khl[h] if idx == 0 else None,
                                       ev_q_done if hj == 0 and idx == 0
                                       else None])

                coff = mw_base[j] + (cap - 1 - rho) * 128
                ev_al[idx] = OP("vector", (lambda eng, b4=b4, coff=coff, h=h:
                    eng.scalar_tensor_tensor(
                        pb[b4][:, 0:QG], mwin_s[:, coff:coff + QG],
                        slopes_l[:, h:h + 1], pb[b4][:, 0:QG],
                        ALU.mult, ALU.add)),
                    waits=[ev_st[idx], ev_mw])

                if idx == 1 and pn is not None:
                    ev_ri_p = norm_recip(pn)

                ev_ex[idx] = OP("scalar", (lambda eng, b4=b4, e4=e4:
                    eng.activation(ETs[e4][:], pb[b4][:, 0:QG], AF.Exp)),
                    waits=[ev_al[idx], et_last_pv[e4]])
                bank_last_exp[b4] = ev_ex[idx]

                if idx == 2 and pn is not None:
                    ev_rb_p = norm_bcast(pn, ev_ri_p)
                if idx == 3 and pn is not None:
                    ev_rc_p = norm_rbc(pn, ev_rb_p)
                if idx == 4 and pn is not None:
                    norm_evac(pn, ev_rc_p, oslot_of[pn[0]])
                    pn = None

                if idx >= PV_LA:
                    emit_pv(idx - PV_LA)
                gidx += 1

            for idx in range(W - PV_LA, W):
                emit_pv(idx)

            pn = (hj, ev_pv[W - 1], pvb, par)
            hj += 1
        ev_head_done[h] = pn[1]

    ev_ri_p = norm_recip(pn)
    ev_rb_p = norm_bcast(pn, ev_ri_p)
    ev_rc_p = norm_rbc(pn, ev_rb_p)
    ev_last_evac = norm_evac(pn, ev_rc_p, oslot_of[pn[0]])

    HQ = H // 4
    ev_woc = []
    for c in range(4):
        ev_woc.append(OP("gpsimd", (lambda eng, c=c: eng.dma_start(
            wo_b[:, c * HQ * E:(c + 1) * HQ * E].rearrange(
                "p (t d) -> p t d", t=HQ),
            woT_d[c * HQ * 128:(c + 1) * HQ * 128, :].rearrange(
                "(t p) d -> p t d", p=128))),
            waits=[ev_last_evac], dma="wo%d" % c))
    ev_wo = ev_woc[0]

    ev_wo_mm = {}
    ev_wo_ev = {}
    ev_sdma = {}
    for j in range(NSLOT):
        for eo in range(FT):
            kk = j * FT + eo
            par = kk % 2

            def p3_wo(eng, j=j, eo=eo, par=par, first=(kk == 0)):
                for ein in range(H):
                    if first and ein % HQ == 0:
                        wc = ev_woc[ein // HQ]
                        eng.wait_ge(sch.sems[wc[0]], wc[1])
                    ins = eng.matmul(
                        pb[6 + par][:, 0:QG],
                        wo_b[:, (ein * FT + eo) * 128:(ein * FT + eo) * 128 + 128],
                        OT_sb[:, (ein * NSLOT + j) * QG:(ein * NSLOT + j) * QG + QG],
                        start=(ein == 0), stop=(ein == H - 1))
                return ins
            ev_wo_mm[kk] = OP("tensor", p3_wo,
                              waits=[ev_last_evac,
                                     ev_woc[3] if kk == 1 else None,
                                     ev_wo_ev.get(kk - 2)])

            ev_wo_ev[kk] = OP("scalar", (lambda eng, eo=eo, par=par:
                eng.activation(sstg[par][:], pb[6 + par][:, 0:QG],
                               AF.Identity, bias=bo_l[:, eo:eo + 1])),
                waits=[ev_wo_mm[kk], ev_sdma.get(kk - 2)])

            ev_sdma[kk] = OP("scalar", (lambda eng, j=j, eo=eo, par=par:
                eng.dma_start(s_dram[eo * 128:eo * 128 + 128,
                                     j * QG:j * QG + QG], sstg[par][:])),
                waits=[ev_wo_ev[kk]], dma="sd%d" % par)
    ev_sd_all = [("sd0", sch.cnt["sd0"], 16), ("sd1", sch.cnt["sd1"], 16)]
    ev_attn_pe = ev_wo_mm[NSLOT * FT - 1]

    ev_xh_evac = {}
    ev_h128mm_last = None
    nch = max(1, FH // 512)
    wch = FH // nch
    for hh in range(2):
        def p4_su(eng, hh=hh):
            for ft in range(FT):
                ins = eng.tensor_scalar(
                    Fbuf[:, ft * FH:(ft + 1) * FH],
                    xnTf[:, ft * TOKF + hh * FH:ft * TOKF + hh * FH + FH],
                    su_ug_l[:, ft:ft + 1], None, ALU.mult)
            return ins
        ev_su = OP("vector", p4_su,
                   waits=[ev_attn_pe, ev_xnt.get(NSLAB - 1), ev_h128mm_last])

        ev_stg = ev_su
        src, dst = Fbuf, F2buf
        for st in range(nst1):
            d = 1 << st
            dd = F2b if st == nst1 - 1 else dst

            def p4_stage(eng, src=src, dd=dd, d=d):
                s4 = src.rearrange("p (g two w) -> p g two w", two=2, w=d * FH)
                d4 = dd.rearrange("p (g two w) -> p g two w", two=2, w=d * FH)
                eng.tensor_add(d4[:, :, 0, :], s4[:, :, 0, :], s4[:, :, 1, :])
                return eng.tensor_sub(d4[:, :, 1, :], s4[:, :, 0, :],
                                      s4[:, :, 1, :])
            ev_stg = OP("vector", p4_stage, waits=[ev_stg])
            src, dst = dst, src

        for ft in range(FT):
            for chi in range(nch):
                k = ft * nch + chi
                gk = hh * FT * nch + k

                def p4_h1(eng, ft=ft, chi=chi, k=k):
                    return eng.matmul(
                        pb[k % 2][:, 0:wch], h128_b[:],
                        F2b[:, ft * FH + chi * wch:ft * FH + chi * wch + wch],
                        start=True, stop=True)
                ev_mm = OP("tensor", p4_h1,
                           waits=[ev_stg, ev_h128, ev_xh_evac.get(gk - 2)])

                def p4_h1e(eng, ft=ft, chi=chi, k=k, hh=hh):
                    return eng.activation(
                        xnTf[:, ft * TOKF + hh * FH + chi * wch:
                             ft * TOKF + hh * FH + chi * wch + wch],
                        pb[k % 2][:, 0:wch], AF.Copy)
                ev_xh_evac[gk] = OP("scalar", p4_h1e, waits=[ev_mm, ev_su])
                ev_h128mm_last = ev_mm
    ev_xh_done = ev_xh_evac[2 * FT * nch - 1]

    ev_ug_ld = {}
    ev_ug_mm = {}
    ev_ug_ev = {}
    ev_u = {}
    nld = 0
    for it in range(IT):
        for ki, wd in ((0, wupT_d), (1, wgT_d)):
            slot = nld % 3
            pn = nld - 3
            prev_mm = ev_ug_mm.get((pn // 2, pn % 2)) if pn >= 0 else None
            ev_ug_ld[(it, ki)] = OP("gpsimd",
                (lambda eng, wd=wd, it=it, slot=slot: eng.dma_start(
                    wring[slot].rearrange("p (t c) -> p t c", t=FT),
                    wd[:, it * 128:it * 128 + 128].rearrange(
                        "(t p) c -> p t c", p=128))),
                waits=[ev_attn_pe, prev_mm], dma="wu%d" % slot)
            nld += 1

            def p4_ug(eng, it=it, ki=ki, slot=slot):
                bk0 = (0 if it % 2 == 0 else 4) + 2 * ki
                for ft in range(FT):
                    for u in range(UPC):
                        ins = eng.matmul(
                            pb[bk0 + u][:, 0:UCH],
                            wring[slot][:, ft * 128:ft * 128 + 128],
                            xnTf[:, ft * TOKF + u * UCH:ft * TOKF + u * UCH + UCH],
                            start=(ft == 0), stop=(ft == FT - 1))
                return ins
            ev_ug_mm[(it, ki)] = OP("tensor", p4_ug,
                                    waits=[ev_ug_ld[(it, ki)], ev_xh_done,
                                           ev_ug_ev.get((it - 2, ki))])

            def p4_ugev(eng, it=it, ki=ki):
                bk0 = (0 if it % 2 == 0 else 4) + 2 * ki
                vec = upvec if ki == 0 else gatevec
                func = AF.Copy if ki == 0 else AF.Silu
                dstb = A_st if ki == 0 else B_st
                for u in range(UPC):
                    ins = eng.activation(dstb[:, u * UCH:(u + 1) * UCH],
                                         pb[bk0 + u][:, 0:UCH], func,
                                         scale=vec[:, it:it + 1])
                return ins
            ev_ug_ev[(it, ki)] = OP("scalar", p4_ugev,
                                    waits=[ev_ug_mm[(it, ki)],
                                           ev_u.get(it - 1)] + ev_sd_all)

        ev_u[it] = OP("vector", (lambda eng, it=it: eng.tensor_mul(
            Uap(it, 0, TOKF), A_st[:], B_st[:])),
            waits=[ev_ug_ev[(it, 0)], ev_ug_ev[(it, 1)], ev_attn_pe])
    ev_u_all = ev_u[IT - 1]

    ev_h2_mm = {}
    ev_h2_ev = {}
    for it in range(IT):
        for u in range(UPC):
            k = it * UPC + u
            if k >= 4:
                wprev = ev_h2_ev[((k - 4) // UPC, (k - 4) % UPC)]
            else:
                wprev = ev_ug_ev[(IT - 2, 1)] if IT >= 2 else None

            def p4_h2(eng, it=it, u=u, k=k):
                return eng.matmul(pb[k % 4][:, 0:UCH], h128_b[:],
                                  Uap(it, u * UCH, UCH), start=True, stop=True)
            ev_h2_mm[(it, u)] = OP("tensor", p4_h2,
                                   waits=[ev_u[it], ev_h128, wprev])

            def p4_h2e(eng, it=it, u=u, k=k):
                return eng.activation(Uap(it, u * UCH, UCH),
                                      pb[k % 4][:, 0:UCH], AF.Copy)
            ev_h2_ev[(it, u)] = OP("scalar", p4_h2e, waits=[ev_h2_mm[(it, u)]])
    ev_h2_last = ev_h2_ev[(IT - 1, UPC - 1)]

    ev_z = None
    ev_zH = None
    for sl in range(NSL):
        sl0 = sl * CS
        ename = "vector"
        cA, cB = C1, C2

        def p4_sA(eng, sl0=sl0, cA=cA):
            q = ITQ
            for half, (ua, ub) in enumerate(((Us[0], Us[2]), (Us[1], Us[3]))):
                a = ua.rearrange("p (t c) -> p t c", c=TOKF)[:, :, sl0:sl0 + CS]
                b = ub.rearrange("p (t c) -> p t c", c=TOKF)[:, :, sl0:sl0 + CS]
                c3 = cA.rearrange("p (t c) -> p t c", c=CS)
                eng.tensor_add(c3[:, half * q:(half + 1) * q, :], a, b)
                ins = eng.tensor_sub(
                    c3[:, 2 * q + half * q:2 * q + (half + 1) * q, :], a, b)
            return ins
        ev_sA = OP(ename, p4_sA,
                   waits=[ev_h2_last if sl == 0 else ev_z,
                          ev_ug_mm[(IT - 1, 1)] if sl == 0 else None])

        ev_stg = ev_sA
        src, dst = cA, cB
        for st in range(nst2 - 2, -1, -1):
            d = 1 << st
            last = (st == 0)

            def p4_s2(eng, src=src, dst=dst, d=d, last=last, sl0=sl0):
                if not last:
                    s4 = src.rearrange("p (g two w) -> p g two w", two=2,
                                       w=d * CS)
                    d4 = dst.rearrange("p (g two w) -> p g two w", two=2,
                                       w=d * CS)
                    eng.tensor_add(d4[:, :, 0, :], s4[:, :, 0, :],
                                   s4[:, :, 1, :])
                    return eng.tensor_sub(d4[:, :, 1, :], s4[:, :, 0, :],
                                          s4[:, :, 1, :])
                s4 = src.rearrange("p (g two w) -> p g two w", two=2, w=CS)
                for uq in range(4):
                    utv = Us[uq].rearrange("p (t two c) -> p t two c", two=2,
                                           c=TOKF)[:, :, :, sl0:sl0 + CS]
                    gg = uq * (ITQ // 2)
                    eng.tensor_add(utv[:, :, 0, :],
                                   s4[:, gg:gg + ITQ // 2, 0, :],
                                   s4[:, gg:gg + ITQ // 2, 1, :])
                    ins = eng.tensor_sub(utv[:, :, 1, :],
                                         s4[:, gg:gg + ITQ // 2, 0, :],
                                         s4[:, gg:gg + ITQ // 2, 1, :])
                return ins
            ev_stg = OP(ename, p4_s2, waits=[ev_stg])
            src, dst = dst, src
        ev_z = ev_stg
        if sl == NSL // 2 - 1:
            ev_zH = ev_z

    ev_wd_ld = {}
    ev_dn_mm = {}
    ev_gs = {}
    ev_comb = {}
    ev_outs = []
    gld = 0
    gc = 0
    NEOG = FT // 4
    FHT = FH
    for ps in range(2):
        tk0 = ps * FHT
        ev_zg = ev_zH if ps == 0 else ev_z
        for eog in range(NEOG):
            g2 = ps * NEOG + eog
            for it4 in range(IT // K2):
                slot = gld % NWDR
                prev_mm = ev_dn_mm.get(gld - NWDR)
                ev_wd_ld[(ps, eog, it4)] = OP("gpsimd",
                    (lambda eng, eog=eog, it4=it4, slot=slot: eng.dma_start(
                        wdring[slot].rearrange("p (b c) -> p b c", b=K2),
                        wdT_d[it4 * K2 * 128:(it4 + 1) * K2 * 128,
                              eog * 512:eog * 512 + 512].rearrange(
                                  "(b p) c -> p b c", p=128))),
                    waits=[ev_u_all, prev_mm], dma="wd%d" % slot)

                def p4_dn(eng, eog=eog, it4=it4, slot=slot, tk0=tk0, g2=g2):
                    bk0 = (g2 % 2) * 4
                    for iti in range(K2):
                        it = it4 * K2 + iti
                        for e4 in range(4):
                            ins = eng.matmul(
                                pb[bk0 + e4][:, 0:FHT],
                                wdring[slot][:, iti * 512 + e4 * 128:
                                             iti * 512 + e4 * 128 + 128],
                                Uap(it, tk0, FHT),
                                start=(it == 0), stop=(it == IT - 1))
                    return ins
                ws = [ev_wd_ld[(ps, eog, it4)]]
                if it4 == 0:
                    ws += [ev_zg]
                    ws += [ev_gs.get((g2 - 2, kkk)) for kkk in range(4)]
                ev_dn_mm[gld] = OP("tensor", p4_dn, waits=ws)
                gld += 1

            for e4 in range(4):
                eo = eog * 4 + e4
                par = gc % 2
                bk = (g2 % 2) * 4 + e4
                prev2 = ev_comb.get(gc - 2)
                prev_out = ev_outs[gc - 1] if gc >= 2 else None

                ev_g = OP("scalar", (lambda eng, bk=bk, eo=eo, par=par:
                    eng.activation(g_st[:, par * FHT:(par + 1) * FHT],
                                   pb[bk][:, 0:FHT], AF.Silu,
                                   scale=dnvec[:, eo:eo + 1])),
                    waits=[ev_dn_mm[gld - 1], prev2])
                ev_gs[(g2, e4)] = ev_g

                ev_ss = OP("scalar", (lambda eng, eo=eo, tk0=tk0, par=par:
                    eng.dma_start(ss_st[:, par * FHT:(par + 1) * FHT],
                                  s_dram[eo * 128:(eo + 1) * 128,
                                         tk0:tk0 + FHT])),
                    waits=ev_sd_all + [prev2, ev_zg], dma="st%d" % par)

                ev_xf = OP("scalar", (lambda eng, eo=eo, tk0=tk0, par=par:
                    eng.dma_start(xf_st[:, par * FHT:(par + 1) * FHT],
                                  xfT_d[eo * 128:(eo + 1) * 128,
                                        tk0:tk0 + FHT])),
                    waits=[prev2, ev_zg], dma="st%d" % par)

                def p4_comb(eng, par=par):
                    sch.chain("vector", eng, eng.tensor_mul(
                        o_st[:, par * FHT:(par + 1) * FHT],
                        g_st[:, par * FHT:(par + 1) * FHT],
                        ss_st[:, par * FHT:(par + 1) * FHT]))
                    return eng.tensor_add(
                        res_st[:, par * FHT:(par + 1) * FHT],
                        o_st[:, par * FHT:(par + 1) * FHT],
                        xf_st[:, par * FHT:(par + 1) * FHT])
                ev_cb = OP("vector", p4_comb,
                           waits=[ev_g, ev_ss, ev_xf, prev_out])
                ev_comb[gc] = ev_cb

                ev_outs.append(OP("sync", (lambda eng, eo=eo, tk0=tk0,
                                           par=par:
                    eng.dma_start(out_d[eo * 128:(eo + 1) * 128,
                                        tk0:tk0 + FHT],
                                  res_st[:, par * FHT:(par + 1) * FHT])),
                    waits=[ev_cb], dma="outd"))
                gc += 1

    n_out = sch.cnt["outd"]
    sch.ops["sync"].append(
        ((lambda eng: eng.nop()), (("outd", n_out, 16),), None))

    blk = nc.Block()
    blk.__enter__()
    sch.emit(blk)
    blk.__exit__(None, None, None)
    return nc


def _host_in_maps(cfg, inputs):
    x = np.ascontiguousarray(inputs["x"], dtype=np.float32)
    h128 = build_h128()
    slopes_all = np.asarray(cfg.slopes, np.float32)
    wqT = np.ascontiguousarray(inputs["Wq"].T)
    wkT = np.ascontiguousarray(inputs["Wk"].T)
    wvT = np.ascontiguousarray(inputs["Wv"].T)
    woT = np.ascontiguousarray(inputs["Wo"].T)
    wupT = np.ascontiguousarray(inputs["W_up"].T)
    wgT = np.ascontiguousarray(inputs["W_gate"].T)
    wdT = np.ascontiguousarray(inputs["W_down"].T)

    in_maps = []
    for c in range(8):
        b, r = c // 4, c % 4
        grps = [cfg.grp(j, r) for j in range(cfg.NSLOT)]
        cols = np.concatenate([np.arange(g * cfg.QG, (g + 1) * cfg.QG)
                               for g in grps])
        xb = x[b]
        in_maps.append({
            "xT": np.ascontiguousarray(xb.T),
            "xfT": np.ascontiguousarray(xb.T[:, cols]),
            "wqT": wqT, "wkT": wkT, "wvT": wvT, "woT": woT,
            "bq": np.ascontiguousarray(inputs["bq"]),
            "bk": np.ascontiguousarray(inputs["bk"]),
            "bv": np.ascontiguousarray(inputs["bv"]),
            "bo": np.ascontiguousarray(inputs["bo"]),
            "slopes": slopes_all,
            "mwin": build_mwin(cfg, r),
            "su_ug": np.ascontiguousarray(inputs["su_ug"]),
            "su_dp": np.ascontiguousarray(inputs["su_dp"]),
            "wsc": np.ascontiguousarray(inputs["wsc"]),
            "wdp": np.ascontiguousarray(inputs["wdp"]),
            "wupT": wupT, "wgT": wgT, "wdT": wdT,
            "h128": h128,
        })
    return in_maps


def run(cfg, inputs):
    nc = build_program(cfg)
    in_maps = _host_in_maps(cfg, inputs)
    res = run_bass_kernel_spmd(nc, in_maps, core_ids=list(range(8)))
    out = np.empty((2, cfg.S, cfg.E), np.float32)
    for c in range(8):
        b, r = c // 4, c % 4
        o = np.asarray(res.results[c]["out"])
        for j in range(cfg.NSLOT):
            g = cfg.grp(j, r)
            out[b, g * cfg.QG:(g + 1) * cfg.QG, :] = \
                o[:, j * cfg.QG:(j + 1) * cfg.QG].T
    return out


def kernel(L0_attn_x, L0_attn_Wq, L0_attn_bq, L0_attn_Wk, L0_attn_bk,
           L0_attn_Wv, L0_attn_bv, L0_attn_Wo, L0_attn_bo,
           L0_ffn_SU_ug, L0_ffn_SU_dp, L0_ffn_Wscale_ug, L0_ffn_Wscale_dp,
           L0_ffn_W_up, L0_ffn_W_gate, L0_ffn_W_down):
    cfg = Cfg(S=4096, E=2048, I=8192)
    inputs = dict(
        x=np.asarray(L0_attn_x, np.float32),
        Wq=np.asarray(L0_attn_Wq, np.float32),
        Wk=np.asarray(L0_attn_Wk, np.float32),
        Wv=np.asarray(L0_attn_Wv, np.float32),
        Wo=np.asarray(L0_attn_Wo, np.float32),
        bq=np.asarray(L0_attn_bq, np.float32),
        bk=np.asarray(L0_attn_bk, np.float32),
        bv=np.asarray(L0_attn_bv, np.float32),
        bo=np.asarray(L0_attn_bo, np.float32),
        su_ug=np.asarray(L0_ffn_SU_ug, np.float32),
        su_dp=np.asarray(L0_ffn_SU_dp, np.float32),
        wsc=np.asarray(L0_ffn_Wscale_ug, np.float32),
        wdp=np.asarray(L0_ffn_Wscale_dp, np.float32),
        W_up=np.asarray(L0_ffn_W_up, np.float32),
        W_gate=np.asarray(L0_ffn_W_gate, np.float32),
        W_down=np.asarray(L0_ffn_W_down, np.float32),
    )
    return run(cfg, inputs)


# version 19 (final)
# speedup vs baseline: 1.1995x; 1.1933x over previous
import math
import numpy as np
import concourse.bass as bass
import concourse.mybir as mybir
from concourse.bass_utils import run_bass_kernel_spmd

F32 = mybir.dt.float32
F16 = mybir.dt.float16
BF16 = mybir.dt.bfloat16
AF = mybir.ActivationFunctionType
ALU = mybir.AluOpType

NEG_BIG = -60000.0
BAND_TH = 90.0


def alibi_slopes(n):
    def pow2_slopes(n):
        start = 2 ** (-(2 ** -(math.log2(n) - 3)))
        return [start * start ** i for i in range(n)]
    if math.log2(n).is_integer():
        return pow2_slopes(n)
    c = 2 ** math.floor(math.log2(n))
    return pow2_slopes(c) + alibi_slopes(2 * c)[::2][: n - c]


class Cfg:
    def __init__(self, S=4096, E=2048, I=8192):
        self.S, self.E, self.I = S, E, I
        self.QG = 256
        self.H = E // 128
        self.FT = E // 128
        self.IT = I // 128
        self.TOKF = S // 4
        self.NGRP = S // self.QG
        self.NSLOT = self.NGRP // 4
        self.SC = 1.0 / math.sqrt(128.0)
        self.CS = 32
        self.slopes = alibi_slopes(self.H)
        self.caps = [(4 * j + 4) * (self.QG // 128)
                     for j in range(self.NSLOT)]
        self.band = [int(math.ceil((BAND_TH / s + self.QG) / 128.0)) + 6
                     for s in self.slopes]
        self.W = [[min(self.caps[j], self.band[h]) for h in range(self.H)]
                  for j in range(self.NSLOT)]
        self.mw_w = [self.caps[j] * 128 + self.QG - 128
                     for j in range(self.NSLOT)]

    def grp(self, j, r):
        return 4 * j + (r if j % 2 == 0 else 3 - r)


def build_mwin(cfg, r):
    cols = []
    for j in range(cfg.NSLOT):
        g0 = cfg.grp(j, r) * cfg.QG
        w = np.arange(cfg.mw_w[j])[None, :]
        p = np.arange(128)[:, None]
        dist = g0 + w - (cfg.caps[j] - 1) * 128 - p
        cols.append(np.where(dist >= 0, -dist, NEG_BIG))
    return np.ascontiguousarray(np.concatenate(cols, axis=1).astype(np.float32))


def build_h128():
    h = np.array([[1.0]], dtype=np.float32)
    for _ in range(7):
        h = np.block([[h, h], [h, -h]])
    return h.astype(np.float32)


class Sched:
    ENGINES = ("sync", "gpsimd", "tensor", "vector", "scalar")

    def __init__(self, nc):
        self.nc = nc
        self.ops = {e: [] for e in self.ENGINES}
        self.sems = {}
        self.cnt = {}
        self.esem = {"tensor": "pe", "vector": "dve", "scalar": "act",
                     "gpsimd": "pool", "sync": "sp"}

    def sem(self, name):
        if name not in self.sems:
            self.sems[name] = self.nc.alloc_semaphore("s_" + name)
            self.cnt[name] = 0
        return self.sems[name]

    def op(self, eng, fn, waits=(), dma=None, inc=True):
        ev = None
        if dma is not None:
            self.sem(dma)
            self.cnt[dma] += 16
            ev = (dma, self.cnt[dma], 16)
        elif inc:
            s = self.esem[eng]
            self.sem(s)
            self.cnt[s] += 1
            ev = (s, self.cnt[s], 1)
        ws = tuple(w for w in waits if w is not None)
        self.ops[eng].append((fn, ws, ev))
        return ev

    def chain(self, eng_name, E, ins):
        s = self.sem(self.esem[eng_name] + "ch")
        self.cnt[self.esem[eng_name] + "ch"] += 1
        v = self.cnt[self.esem[eng_name] + "ch"]
        ins.then_inc(self.sems[self.esem[eng_name] + "ch"], 1)
        E.wait_ge(self.sems[self.esem[eng_name] + "ch"], v)

    def emit(self, blk):
        for eng in self.ENGINES:
            lst = self.ops[eng]
            if not lst:
                continue

            def body(E, lst=lst):
                done = {}
                for fn, ws, ev in lst:
                    need = {}
                    for (s, v, _b) in ws:
                        if v > need.get(s, 0):
                            need[s] = v
                    for s, v in need.items():
                        if v > done.get(s, 0):
                            E.wait_ge(self.sems[s], v)
                            done[s] = v
                    ins = fn(E)
                    if ev is not None:
                        ins.then_inc(self.sems[ev[0]], ev[2])
            getattr(blk, eng)(body)


def build_program(cfg):
    S, E, I, QG = cfg.S, cfg.E, cfg.I, cfg.QG
    FT, IT, H = cfg.FT, cfg.IT, cfg.H
    NSLOT, CS = cfg.NSLOT, cfg.CS
    TOKF = cfg.TOKF
    NSLAB_A = S // 128
    NSLAB_F = TOKF // 128
    NSLAB = NSLAB_A + NSLAB_F
    FH = TOKF // 2
    ITQ = IT // 4
    THN = max(1, TOKF // 512)
    TCH = TOKF // THN
    UPC = max(1, TOKF // 512)
    UCH = TOKF // UPC
    nst1 = int(math.log2(FT))
    nst2 = int(math.log2(IT))
    NSL = TOKF // CS
    KT = S // 128
    VCH = E // 512
    QCH = TOKF // 512 if TOKF >= 512 else 1
    QW = min(512, TOKF)
    MWTOT = sum(cfg.mw_w)

    nc = bass.Bass("TRN2", target_bir_lowering=False)
    din = lambda n, sh: nc.dram_tensor(n, sh, F32, kind="ExternalInput").ap()
    xT_d = din("xT", [E, S])
    xfT_d = din("xfT", [E, TOKF])
    wqT_d = din("wqT", [E, E])
    wkT_d = din("wkT", [E, E])
    wvT_d = din("wvT", [E, E])
    woT_d = din("woT", [E, E])
    bq_d = din("bq", [E])
    bk_d = din("bk", [E])
    bv_d = din("bv", [E])
    bo_d = din("bo", [E])
    slopes_d = din("slopes", [H])
    mwin_d = din("mwin", [128, MWTOT])
    su_ug_d = din("su_ug", [E])
    su_dp_d = din("su_dp", [I])
    wsc_d = din("wsc", [2 * I])
    wdp_d = din("wdp", [E])
    wupT_d = din("wupT", [E, I])
    wgT_d = din("wgT", [E, I])
    wdT_d = din("wdT", [I, E])
    h128_d = din("h128", [128, 128])
    out_d = nc.dram_tensor("out", [E, TOKF], F32, kind="ExternalOutput").ap()
    k_dram = nc.dram_tensor("k_scr", [H, 128, S], BF16, kind="Internal").ap()
    v_dram = nc.dram_tensor("v_scr", [S, E], BF16, kind="Internal").ap()
    s_dram = nc.dram_tensor("s_scr", [E, TOKF], BF16, kind="Internal").ap()

    BASE0 = ((int(nc.sbuf_base) + 63) // 64) * 64

    def at(name, shape, dt, o):
        return nc.alloc_sbuf_tensor_at(name, shape, dt, offset=BASE0 + o).ap()

    _o = [0]
    def cst(name, shape, dt):
        o = (_o[0] + 63) // 64 * 64
        ap = at(name, shape, dt, o)
        _o[0] = o + int(np.prod(shape[1:])) * mybir.dt.size(dt)
        return ap

    ones1_b = cst("ones1_b", [1, 128], BF16)
    ones128_b = cst("ones128_b", [128, 1], BF16)
    ones1_f = cst("ones1_f", [1, 128], F32)
    h128_b = cst("h128_b", [128, 128], BF16)
    su_ug_l = cst("su_ug_l", [128, FT], F32)
    su_dp_l = cst("su_dp_l", [128, IT], F32)
    upvec = cst("upvec", [128, IT], F32)
    gatevec = cst("gatevec", [128, IT], F32)
    dnvec = cst("dnvec", [128, FT], F32)
    bq_l = cst("bq_l", [128, H], F32)
    bk_l = cst("bk_l", [128, H], F32)
    bo_l = cst("bo_l", [128, FT], F32)
    slopes_r = cst("slopes_r", [1, H], F32)
    slopes_l = cst("slopes_l", [128, H], F32)
    bvB = cst("bvB", [128, E], BF16)
    m_row = cst("m_row", [1, 256], F32)
    sq_row = cst("sq_row", [1, 256], F32)
    msq_row = cst("msq_row", [1, 256], F32)
    rstd_row = cst("rstd_row", [1, 256], F32)
    var_row = cst("var_row", [1, 256], F32)
    mrstd_row = cst("mrstd_row", [1, 256], F32)
    rstdB = cst("rstdB", [128, 256], F32)
    mrstdB = cst("mrstdB", [128, 256], F32)
    xn_scr = cst("xn_scr", [128, 128], F32)
    kstg = [cst(f"kstg{i}", [128, H * 128], BF16) for i in range(2)]
    vstg = [cst(f"vstg{i}", [128, 512], BF16) for i in range(2)]
    sstg = [cst(f"sstg{i}", [128, QG], BF16) for i in range(2)]
    R0 = ((_o[0] + 63) // 64) * 64

    sz_U4 = ITQ * TOKF * 2
    szA = max(FT * E * 2, MWTOT * 2 + 4 * S * 2 + 4 * QG * 2 + 4 * QG * 4,
              2 * FT * FH * 4, 2 * sz_U4)
    szB = max(FT * E * 2, H * TOKF * 2 + H * NSLOT * QG * 2, 2 * sz_U4)
    e_stage = 2 * IT * CS * 4 + 4 * FH * 2 + 6 * FH * 4
    szC = max(FT * TOKF * 2, e_stage)
    szD = max(5 * FT * 256,
              E * 4,
              3 * FT * 256 + 2 * TOKF * 4,
              3 * 1024)
    A = R0
    B = A + szA
    Cr = B + szB
    D = Cr + szC
    TOP = D + szD
    LIMIT = 229300 - BASE0 - 1024
    assert TOP <= LIMIT, (
        f"sbuf over: {TOP} > {LIMIT} (A {szA} B {szB} C {szC} D {szD} R0 {R0})")

    wk_b = at("wk_b", [128, FT * E], BF16, A)
    wq_b = at("wq_b", [128, FT * E], BF16, A)
    wo_b = at("wo_b", [128, FT * E], BF16, A)
    mwin_s = at("mwin_s", [128, MWTOT], F16, A)
    kh = [at(f"kh{i}", [128, S], BF16, A + MWTOT * 2 + i * S * 2)
          for i in range(2)]
    vh = [at(f"vh{i}", [128, KT * 128], BF16, A + MWTOT * 2 + (2 + i) * S * 2)
          for i in range(2)]
    ETs = [at(f"ET{i}", [128, QG], BF16, A + MWTOT * 2 + 4 * S * 2 + i * QG * 2)
           for i in range(4)]
    _pA = A + MWTOT * 2 + 4 * S * 2 + 4 * QG * 2
    rinv2 = at("rinv2", [1, 2 * QG], F32, _pA)
    rbB2 = at("rbB2", [128, 2 * QG], F32, _pA + 2 * QG * 4)
    assert _pA + 2 * QG * 4 + 2 * QG * 4 <= A + szA
    Fbuf = at("Fbuf", [128, FT * FH], F32, A)
    F2buf = at("F2buf", [128, FT * FH], F32, A + FT * FH * 4)
    F2b = at("F2b", [128, FT * FH], BF16,
             A if nst1 % 2 == 0 else A + FT * FH * 4)
    wv_b = at("wv_b", [128, FT * E], BF16, B)
    q_sb = at("q_sb", [128, H * TOKF], BF16, B)
    OT_sb = at("OT_sb", [128, H * NSLOT * QG], BF16, B + H * TOKF * 2)
    Us = [at("U0", [128, ITQ * TOKF], BF16, A),
          at("U1", [128, ITQ * TOKF], BF16, A + sz_U4),
          at("U2", [128, ITQ * TOKF], BF16, B),
          at("U3", [128, ITQ * TOKF], BF16, B + sz_U4)]
    xnTf = at("xnTf", [128, FT * TOKF], BF16, Cr)
    C1 = at("C1", [128, IT * CS], F32, Cr)
    C2 = at("C2", [128, IT * CS], F32, Cr + IT * CS * 4)
    p4c = Cr + 2 * IT * CS * 4
    g_st = at("g_st", [128, 2 * FH], BF16, p4c)
    ss_st = at("ss_st", [128, 2 * FH], BF16, p4c + FH * 4)
    xf_st = at("xf_st", [128, 2 * FH], F32, p4c + FH * 8)
    o_st = at("o_st", [128, 2 * FH], F32, p4c + FH * 16)
    res_st = at("res_st", [128, 2 * FH], F32, p4c + FH * 24)
    xTb = [at(f"xTb{i}", [128, FT * 128], BF16, D + i * FT * 256)
           for i in range(2)]
    xnT2 = [at(f"xnT2{i}", [128, FT * 128], BF16, D + (2 + i) * FT * 256)
            for i in range(2)]
    sq_scr = at("sq_scr", [128, FT * 128], BF16, D + 4 * FT * 256)
    bv_r = at("bv_r", [1, E], F32, D)
    wring = [at(f"wring{i}", [128, FT * 128], BF16, D + i * FT * 256)
             for i in range(3)]
    A_st = at("A_st", [128, TOKF], F32, D + 3 * FT * 256)
    B_st = at("B_st", [128, TOKF], F32, D + 3 * FT * 256 + TOKF * 4)
    K2 = next(k for k in (4, 2, 1)
              if IT % k == 0 and 4 * k * 1024 <= szD)
    NWDR = 5 if 5 * K2 * 1024 <= szD else 4
    wdring = [at(f"wdring{i}", [128, K2 * 512], BF16, D + i * K2 * 1024)
              for i in range(NWDR)]
    assert NWDR * K2 * 1024 <= szD

    def Uap(it, c0, w):
        t = Us[it // ITQ]
        base = (it % ITQ) * TOKF
        return t[:, base + c0: base + c0 + w]

    pb = [nc.alloc_psum_tensor(f"pb{i}", [128, 512], F32).ap() for i in range(8)]
    nc.sbuf_base = BASE0 + ((TOP + 63) // 64) * 64

    sch = Sched(nc)
    OP = sch.op

    c0_evs = []
    def mk_cdma(dst, src):
        return lambda eng, dst=dst, src=src: eng.dma_start(
            dst[:], src, allow_slow_non_contiguous=True)
    for dst, src in [
        (su_ug_l, su_ug_d.rearrange("(t p) -> p t", p=128)),
        (su_dp_l, su_dp_d.rearrange("(t p) -> p t", p=128)),
        (upvec, wsc_d[0:I].rearrange("(t p) -> p t", p=128)),
        (gatevec, wsc_d[I:2 * I].rearrange("(t p) -> p t", p=128)),
        (dnvec, wdp_d.rearrange("(t p) -> p t", p=128)),
        (bq_l, bq_d.rearrange("(t p) -> p t", p=128)),
        (bk_l, bk_d.rearrange("(t p) -> p t", p=128)),
        (bo_l, bo_d.rearrange("(t p) -> p t", p=128)),
        (slopes_r, slopes_d[None, :]),
        (bv_r, bv_d[None, :]),
    ]:
        c0_evs.append(OP("sync", mk_cdma(dst, src), dma="c0"))
    ev_c0 = c0_evs[-1]

    ev_h128 = OP("gpsimd", lambda eng: eng.dma_start(h128_b[:], h128_d[:]),
                 dma="cg")
    OP("gpsimd", lambda eng: eng.memset(ones1_b[:], 1.0), inc=False)
    OP("gpsimd", lambda eng: eng.memset(ones128_b[:], 1.0), inc=False)
    ev_ones = OP("gpsimd", lambda eng: eng.memset(ones1_f[:], 1.0))
    ev_wk = OP("gpsimd", lambda eng: eng.dma_start(
        wk_b.rearrange("p (t d) -> p t d", t=FT),
        wkT_d.rearrange("(t p) d -> p t d", p=128)), dma="wkd")
    ev_wv = OP("gpsimd", lambda eng: eng.dma_start(
        wv_b.rearrange("p (t d) -> p t d", t=FT),
        wvT_d.rearrange("(t p) d -> p t d", p=128)), dma="wvd")

    def p0_vec(eng):
        inv1 = 1.0 / math.sqrt(E)
        sch.chain("vector", eng,
                  eng.tensor_scalar(upvec[:], upvec[:], inv1, None, ALU.mult))
        sch.chain("vector", eng,
                  eng.tensor_mul(upvec[:], upvec[:], su_dp_l[:]))
        eng.tensor_scalar(gatevec[:], gatevec[:], inv1, None, ALU.mult)
        eng.tensor_scalar(dnvec[:], dnvec[:], 1.0 / math.sqrt(I), None,
                          ALU.mult)
        return eng.tensor_scalar(bq_l[:], bq_l[:], cfg.SC, None, ALU.mult)
    ev_vec0 = OP("vector", p0_vec, waits=c0_evs)

    ev_pe0 = OP("tensor", (lambda eng: eng.matmul(
        pb[6][:, 0:H], ones1_f[:], slopes_r[:], start=True, stop=True)),
        waits=[ev_c0, ev_ones] + c0_evs[-2:])
    ev_slo = OP("scalar", (lambda eng: eng.copy(slopes_l[:], pb[6][:, 0:H])),
                waits=[ev_pe0])

    ev_bvB = None
    for ch in range(E // 512):
        ev_m = OP("tensor", (lambda eng, ch=ch: eng.matmul(
            pb[7][:, 0:512], ones1_f[:], bv_r[0:1, ch * 512:(ch + 1) * 512],
            start=True, stop=True)), waits=[ev_pe0, ev_bvB])
        ev_bvB = OP("scalar", (lambda eng, ch=ch: eng.copy(
            bvB[:, ch * 512:(ch + 1) * 512], pb[7][:, 0:512])), waits=[ev_m])

    ev_ld = {}
    ev_bcE = {}
    ev_sq = {}
    ev_sqmm = {}
    ev_rows = {}
    ev_rstd = {}
    ev_bcast_ev = {}
    ev_cons_all = {}
    ev_xnt = {}
    ev_kdma = {}
    ev_vdma = {}
    ev_kevac = {}
    ev_vevac = {}

    emit_order = []

    def ln_head(slab, li):
        is_ffn = slab >= NSLAB_A
        cc0 = (slab - NSLAB_A) * 128 if is_ffn else slab * 128
        src_t = xfT_d if is_ffn else xT_d
        par = li % 2
        psem = "xt%d" % par
        prev = emit_order[li - 1] if li >= 1 else None
        pprev = emit_order[li - 2] if li >= 2 else None

        ev_ld[slab] = OP("gpsimd", (lambda eng, s=src_t, c=cc0, p=par:
            eng.dma_start(xTb[p].rearrange("p (t c) -> p t c", t=FT),
                          s[:, c:c + 128].rearrange("(t p) c -> p t c",
                                                    p=128))),
            waits=[ev_xnt.get(pprev),
                   ev_bvB if li <= 1 else None], dma=psem)

        ev_sq[slab] = OP("vector", (lambda eng, par=par: eng.tensor_mul(
            sq_scr[:], xTb[par][:], xTb[par][:])),
            waits=[ev_ld[slab], ev_sqmm.get(prev)])

        def p1_stats(eng, par=par, ev_s=ev_sq[slab]):
            for ft in range(FT):
                eng.matmul(pb[0][0:1, 0:128], ones128_b[:],
                           xTb[par][:, ft * 128:ft * 128 + 128],
                           start=(ft == 0), stop=(ft == FT - 1))
            eng.wait_ge(sch.sems[ev_s[0]], ev_s[1])
            for ft in range(FT):
                ins = eng.matmul(pb[0][0:1, 128:256], ones128_b[:],
                                 sq_scr[:, ft * 128:ft * 128 + 128],
                                 start=(ft == 0), stop=(ft == FT - 1))
            return ins
        ev_sqmm[slab] = OP("tensor", p1_stats,
                           waits=[ev_ld[slab], ev_rows.get(prev)])

        pr = lambda t, q=par: t[0:1, q * 128:q * 128 + 128]

        def p1_rows(eng, par=par):
            eng.activation(pr(m_row), pb[0][0:1, 0:128], AF.Copy,
                           scale=1.0 / E)
            return eng.activation(pr(sq_row), pb[0][0:1, 128:256], AF.Copy,
                                  scale=1.0 / E)
        ev_rows[slab] = OP("scalar", p1_rows,
                           waits=[ev_sqmm[slab], ev_rstd.get(pprev)])

        def p1_var(eng, par=par):
            sch.chain("vector", eng,
                      eng.tensor_mul(pr(msq_row), pr(m_row), pr(m_row)))
            return eng.scalar_tensor_tensor(pr(var_row), pr(sq_row), 1e-5,
                                            pr(msq_row), ALU.add,
                                            ALU.subtract)
        ev_var = OP("vector", p1_var, waits=[ev_rows[slab]])

        ev_sd = OP("scalar", (lambda eng, par=par: eng.activation(
            pr(msq_row), pr(var_row), AF.Sqrt)), waits=[ev_var])

        def p1_rstd(eng, par=par):
            sch.chain("vector", eng,
                      eng.reciprocal(pr(rstd_row), pr(msq_row)))
            return eng.tensor_mul(pr(mrstd_row), pr(m_row), pr(rstd_row))
        ev_rstd[slab] = OP("vector", p1_rstd,
                           waits=[ev_sd, ev_bcE.get(pprev)])

    def ln_tail(slab, li):
        is_ffn = slab >= NSLAB_A
        cc0 = (slab - NSLAB_A) * 128 if is_ffn else slab * 128
        par = li % 2
        prev = emit_order[li - 1] if li >= 1 else None
        pprev = emit_order[li - 2] if li >= 2 else None
        pr = lambda t, q=par: t[0:1, q * 128:q * 128 + 128]

        def p1_bc(eng, par=par):
            eng.matmul(pb[1][:, 0:128], ones1_f[:], pr(rstd_row), start=True,
                       stop=True)
            return eng.matmul(pb[1][:, 128:256], ones1_f[:], pr(mrstd_row),
                              start=True, stop=True)
        ev_bc = OP("tensor", p1_bc,
                   waits=[ev_rstd[slab], ev_bcast_ev.get(prev), ev_ones])
        ev_bcE[slab] = ev_bc

        def p1_bcev(eng, par=par):
            eng.copy(rstdB[:, par * 128:par * 128 + 128], pb[1][:, 0:128])
            return eng.copy(mrstdB[:, par * 128:par * 128 + 128],
                            pb[1][:, 128:256])
        ev_bcast_ev[slab] = OP("scalar", p1_bcev,
                               waits=[ev_bc, ev_xnt.get(pprev)])

        def p1_xnt(eng, par=par, is_ffn=is_ffn, cc0=cc0):
            for ft in range(FT):
                sch.chain("vector", eng, eng.tensor_mul(
                    xn_scr[:], xTb[par][:, ft * 128:ft * 128 + 128],
                    rstdB[:, par * 128:par * 128 + 128]))
                dst = (xnTf[:, ft * TOKF + cc0: ft * TOKF + cc0 + 128]
                       if is_ffn else xnT2[par][:, ft * 128:ft * 128 + 128])
                ins = eng.tensor_sub(dst, xn_scr[:],
                                     mrstdB[:, par * 128:par * 128 + 128])
                if ft < FT - 1:
                    sch.chain("vector", eng, ins)
            return ins
        ev_xnt[slab] = OP("vector", p1_xnt,
                          waits=[ev_bcast_ev[slab], ev_vec0,
                                 ev_xnt.get(prev),
                                 ev_cons_all.get(slab - 2)
                                 if not is_ffn else None])
        if is_ffn:
            ev_cons_all[slab] = ev_sqmm[slab]

    def emit_kv(slab, par2, tail_cb):
        par = slab % 2
        ev_kc = {}
        for hch in range(H):
            def p1_k(eng, hch=hch, par2=par2):
                bank = pb[2 + (hch % 2)]
                for ft in range(FT):
                    ins = eng.matmul(
                        bank[:, 0:128],
                        wk_b[:, (ft * H + hch) * 128:(ft * H + hch) * 128 + 128],
                        xnT2[par2][:, ft * 128:ft * 128 + 128],
                        start=(ft == 0), stop=(ft == FT - 1))
                return ins
            wprev = (ev_kevac.get((slab, hch - 2)) if hch >= 2
                     else ev_kevac.get((slab - 1, H - 2 + (hch % 2))))
            ev_kc[hch] = OP("tensor", p1_k,
                            waits=[ev_xnt[slab], ev_wk, wprev])
            if hch == 1:
                tail_cb()

            def p1_kev(eng, hch=hch, par=par):
                bank = pb[2 + (hch % 2)]
                return eng.activation(
                    kstg[par][:, hch * 128:hch * 128 + 128],
                    bank[:, 0:128], AF.Identity,
                    bias=bk_l[:, hch:hch + 1])
            ev_kevac[(slab, hch)] = OP(
                "scalar", p1_kev,
                waits=[ev_kc[hch],
                       ev_kdma.get(slab - 2) if hch == 0 else None])

        def p1_kdma(eng, slab=slab, par=par):
            dst = k_dram[:, :, slab * 128:slab * 128 + 128].rearrange(
                "h d t -> d h t")
            return eng.dma_start(dst,
                                 kstg[par].rearrange("p (h t) -> p h t",
                                                     h=H))
        ev_kdma[slab] = OP("scalar", p1_kdma,
                           waits=[ev_kevac[(slab, H - 1)]],
                           dma="kd%d" % par)

        ev_vc = {}
        for vch in range(VCH):
            def p1_v(eng, vch=vch, par2=par2):
                bank = pb[4 + (vch % 2)]
                for ft in range(FT):
                    ins = eng.matmul(
                        bank[:, 0:512],
                        xnT2[par2][:, ft * 128:ft * 128 + 128],
                        wv_b[:, ft * E + vch * 512:ft * E + vch * 512 + 512],
                        start=(ft == 0), stop=(ft == FT - 1))
                return ins
            wprev = (ev_vevac.get((slab, vch - 2)) if vch >= 2
                     else ev_vevac.get((slab - 1, VCH - 2 + (vch % 2))))
            ev_vc[vch] = OP("tensor", p1_v,
                            waits=[ev_xnt[slab], ev_wv, wprev])

            def p1_vev(eng, vch=vch):
                bank = pb[4 + (vch % 2)]
                return eng.tensor_add(vstg[vch % 2][:], bank[:, 0:512],
                                      bvB[:, vch * 512:vch * 512 + 512])
            ev_vevac[(slab, vch)] = OP(
                "vector", p1_vev,
                waits=[ev_vc[vch], ev_bvB,
                       ev_vdma.get((slab, vch - 2)),
                       ev_vdma.get((slab - 1, VCH - 2 + (vch % 2)))])

            def p1_vdma(eng, slab=slab, vch=vch):
                dst = v_dram[slab * 128:slab * 128 + 128,
                             vch * 512:vch * 512 + 512]
                return eng.dma_start(dst, vstg[vch % 2][:])
            ev_vdma[(slab, vch)] = OP("scalar", p1_vdma,
                                      waits=[ev_vevac[(slab, vch)]],
                                      dma="vd%d" % (vch % 2))
        ev_cons_all[slab] = ev_vc[VCH - 1]

    step = NSLAB_A // max(1, NSLAB_F // 2)
    fi = NSLAB_A
    for a in range(NSLAB_A):
        emit_order.append(a)
        if (a + 1) % step == 0 and fi < NSLAB:
            emit_order.append(fi)
            emit_order.append(fi + 1)
            fi += 2
    while fi < NSLAB:
        emit_order.append(fi)
        fi += 1
    pos_attn = {s: p for p, s in enumerate(emit_order) if s < NSLAB_A}
    pos_of = {s: p for p, s in enumerate(emit_order)}

    tailq = []
    li = [0]

    def pop_tail():
        s, lpos = tailq.pop(0)
        ln_tail(s, lpos)

    def pump(limit):
        while li[0] < NSLAB and li[0] <= limit:
            while tailq and tailq[0][1] <= li[0] - 2:
                pop_tail()
            s = emit_order[li[0]]
            ln_head(s, li[0])
            tailq.append((s, li[0]))
            li[0] += 1

    pump(1)
    pop_tail()
    for ki in range(NSLAB_A):
        pump(min(NSLAB - 1, pos_attn[ki] + 2))

        def tail_cb(ki=ki):
            while tailq and (tailq[0][0] >= NSLAB_A
                             or tailq[0][0] <= ki + 1):
                pop_tail()
        emit_kv(ki, pos_attn[ki] % 2, tail_cb)
    pump(NSLAB - 1)
    while tailq:
        pop_tail()

    ev_p1_pe = ev_cons_all[NSLAB - 1]
    ev_spills = [(s, sch.cnt[s], 16) for s in ("kd0", "kd1", "vd0", "vd1")]

    ev_wq = OP("gpsimd", lambda eng: eng.dma_start(
        wq_b.rearrange("p (t d) -> p t d", t=FT),
        wqT_d.rearrange("(t p) d -> p t d", p=128)),
        waits=[ev_cons_all[NSLAB_A - 1]], dma="wqd")

    ev_qc = {}
    ev_qevac = {}
    for hch in range(H):
        for u in range(QCH):
            k2 = hch * QCH + u

            def p2_q(eng, hch=hch, u=u, k2=k2):
                bank = pb[2 + (k2 % 2)]
                for ft in range(FT):
                    ins = eng.matmul(
                        bank[:, 0:QW],
                        wq_b[:, (ft * H + hch) * 128:(ft * H + hch) * 128 + 128],
                        xnTf[:, ft * TOKF + u * QW:ft * TOKF + u * QW + QW],
                        start=(ft == 0), stop=(ft == FT - 1))
                return ins
            xn_need = NSLAB_A + min(NSLAB_F, 4 * (u + 1)) - 1
            lastk = H - 2 + ((H - 2) % 2 != k2 % 2)
            ev_qc[k2] = OP("tensor", p2_q,
                           waits=[ev_wq, ev_xnt[xn_need],
                                  ev_qevac.get(k2 - 2),
                                  ev_kevac[(NSLAB_A - 1, lastk)]
                                  if k2 <= 1 else None])

            def p2_qe(eng, hch=hch, u=u, k2=k2):
                bank = pb[2 + (k2 % 2)]
                return eng.activation(
                    q_sb[:, hch * TOKF + u * QW:hch * TOKF + u * QW + QW],
                    bank[:, 0:QW], AF.Identity, bias=bq_l[:, hch:hch + 1],
                    scale=cfg.SC)
            ev_qevac[k2] = OP("scalar", p2_qe, waits=[ev_qc[k2]])
    ev_q_done = ev_qevac[H * QCH - 1]

    ev_mw = OP("gpsimd", lambda eng: eng.dma_start(mwin_s[:], mwin_d[:]),
               waits=[ev_q_done], dma="mwd")

    mw_base = []
    acc = 0
    for j in range(NSLOT):
        mw_base.append(acc)
        acc += cfg.mw_w[j]

    PV_LA = 3
    bank_last_exp = [None] * 4
    et_last_pv = [None] * 4
    ev_khl = {}
    ev_vhl = {}
    ev_head_done = {}
    ev_evac = {}
    ev_bcast = {}
    gidx = 0
    hj = 0
    pn = None

    def norm_recip(pn):
        _hj, pvl, pvb, par = pn
        return OP("vector", (lambda eng, par=par: eng.reciprocal(
            rinv2[0:1, par * QG:par * QG + QG], pb[6 + par][0:1, 0:QG])),
            waits=[pvl, ev_bcast.get(pn[0] - 2)])

    def norm_bcast(pn, ev_ri):
        _hj, pvl, pvb, par = pn
        ev_bcast[_hj] = OP("tensor", (lambda eng, par=par:
            eng.matmul(pb[6 + par][:, 0:QG], ones1_f[:],
                       rinv2[0:1, par * QG:par * QG + QG],
                       start=True, stop=True)), waits=[ev_ri])
        return ev_bcast[_hj]

    def norm_rbc(pn, ev_rb):
        _hj, pvl, pvb, par = pn
        return OP("scalar", (lambda eng, par=par: eng.copy(
            rbB2[:, par * QG:par * QG + QG], pb[6 + par][:, 0:QG])),
            waits=[ev_rb])

    def norm_evac(pn, ev_rc, oslot):
        _hj, pvl, pvb, par = pn
        ev_evac[_hj] = OP("vector", (lambda eng, pvb=pvb, par=par,
                                     oslot=oslot:
            eng.tensor_mul(OT_sb[:, oslot * QG:oslot * QG + QG],
                           pvb[:, 0:QG], rbB2[:, par * QG:par * QG + QG])),
            waits=[ev_rc])
        return ev_evac[_hj]

    oslot_of = {}
    for h in range(H):
        hp = h % 2
        ev_khl[h] = OP("sync", (lambda eng, h=h, hp=hp: eng.dma_start(
            kh[hp][:], k_dram[h, :, :])),
            waits=ev_spills + [ev_head_done.get(h - 2),
                   ev_mw if h <= 1 else None], dma="kh%d" % hp)
        ev_vhl[h] = OP("sync", (lambda eng, h=h, hp=hp: eng.dma_start(
            vh[hp].rearrange("p (t d) -> p t d", t=KT),
            v_dram[:, h * 128:h * 128 + 128].rearrange("(t p) d -> p t d",
                                                       p=128))),
            waits=ev_spills + [ev_head_done.get(h - 2),
                   ev_mw if h <= 1 else None], dma="vh%d" % hp)

        for j in range(NSLOT):
            cap = cfg.caps[j]
            W = cfg.W[j][h]
            assert W > PV_LA + 2
            q_ap = q_sb[:, h * TOKF + j * QG: h * TOKF + j * QG + QG]
            par = hj % 2
            pvb = pb[4 + par]
            oslot_of[hj] = h * NSLOT + j

            ev_st = {}
            ev_al = {}
            ev_ex = {}
            ev_pv = {}
            ev_ri_p = None
            g0 = gidx

            def emit_pv(idx, g0=g0, cap=cap, W=W, hp=hp, pvb=pvb, h=h):
                rho = cap - W + idx
                e4 = (g0 + idx) % 4

                def fn(eng, rho=rho, e4=e4, idx=idx, W=W, hp=hp, pvb=pvb,
                       par=par):
                    eng.matmul(pvb[:, 0:QG],
                               vh[hp][:, rho * 128:rho * 128 + 128],
                               ETs[e4][:], start=(idx == 0),
                               stop=(idx == W - 1))
                    return eng.matmul(pb[6 + par][0:1, 0:QG], ones128_b[:],
                                      ETs[e4][:], start=(idx == 0),
                                      stop=(idx == W - 1))
                ev_pv[idx] = OP("tensor", fn,
                                waits=[ev_ex[idx],
                                       ev_evac.get(hj - 2) if idx == 0
                                       else None,
                                       ev_vhl[h] if idx == 0 else None])
                et_last_pv[e4] = ev_pv[idx]

            for idx in range(W):
                rho = cap - W + idx
                b4 = gidx % 4
                e4 = gidx % 4

                def mk_st(rho=rho, b4=b4, q_ap=q_ap, hp=hp):
                    def fn(eng):
                        return eng.matmul(pb[b4][:, 0:QG],
                                          kh[hp][:, rho * 128:rho * 128 + 128],
                                          q_ap, start=True, stop=True)
                    return fn
                ev_st[idx] = OP("tensor", mk_st(),
                                waits=[bank_last_exp[b4],
                                       ev_khl[h] if idx == 0 else None,
                                       ev_q_done if hj == 0 and idx == 0
                                       else None])

                coff = mw_base[j] + (cap - 1 - rho) * 128
                ev_al[idx] = OP("vector", (lambda eng, b4=b4, coff=coff, h=h:
                    eng.scalar_tensor_tensor(
                        pb[b4][:, 0:QG], mwin_s[:, coff:coff + QG],
                        slopes_l[:, h:h + 1], pb[b4][:, 0:QG],
                        ALU.mult, ALU.add)),
                    waits=[ev_st[idx], ev_mw])

                if idx == 1 and pn is not None:
                    ev_ri_p = norm_recip(pn)

                ev_ex[idx] = OP("scalar", (lambda eng, b4=b4, e4=e4:
                    eng.activation(ETs[e4][:], pb[b4][:, 0:QG], AF.Exp)),
                    waits=[ev_al[idx], et_last_pv[e4]])
                bank_last_exp[b4] = ev_ex[idx]

                if idx == 2 and pn is not None:
                    ev_rb_p = norm_bcast(pn, ev_ri_p)
                if idx == 3 and pn is not None:
                    ev_rc_p = norm_rbc(pn, ev_rb_p)
                if idx == 4 and pn is not None:
                    norm_evac(pn, ev_rc_p, oslot_of[pn[0]])
                    pn = None

                if idx >= PV_LA:
                    emit_pv(idx - PV_LA)
                gidx += 1

            for idx in range(W - PV_LA, W):
                emit_pv(idx)

            pn = (hj, ev_pv[W - 1], pvb, par)
            hj += 1
        ev_head_done[h] = pn[1]

    ev_ri_p = norm_recip(pn)
    ev_rb_p = norm_bcast(pn, ev_ri_p)
    ev_rc_p = norm_rbc(pn, ev_rb_p)
    ev_last_evac = norm_evac(pn, ev_rc_p, oslot_of[pn[0]])

    ev_wo = OP("gpsimd", lambda eng: eng.dma_start(
        wo_b.rearrange("p (t d) -> p t d", t=FT),
        woT_d.rearrange("(t p) d -> p t d", p=128)),
        waits=[ev_last_evac], dma="wod")

    ev_wo_mm = {}
    ev_wo_ev = {}
    ev_sdma = {}
    for j in range(NSLOT):
        for eo in range(FT):
            kk = j * FT + eo
            par = kk % 2

            def p3_wo(eng, j=j, eo=eo, par=par):
                for ein in range(H):
                    ins = eng.matmul(
                        pb[6 + par][:, 0:QG],
                        wo_b[:, (ein * FT + eo) * 128:(ein * FT + eo) * 128 + 128],
                        OT_sb[:, (ein * NSLOT + j) * QG:(ein * NSLOT + j) * QG + QG],
                        start=(ein == 0), stop=(ein == H - 1))
                return ins
            ev_wo_mm[kk] = OP("tensor", p3_wo,
                              waits=[ev_last_evac, ev_wo,
                                     ev_wo_ev.get(kk - 2)])

            ev_wo_ev[kk] = OP("scalar", (lambda eng, eo=eo, par=par:
                eng.activation(sstg[par][:], pb[6 + par][:, 0:QG],
                               AF.Identity, bias=bo_l[:, eo:eo + 1])),
                waits=[ev_wo_mm[kk], ev_sdma.get(kk - 2)])

            ev_sdma[kk] = OP("scalar", (lambda eng, j=j, eo=eo, par=par:
                eng.dma_start(s_dram[eo * 128:eo * 128 + 128,
                                     j * QG:j * QG + QG], sstg[par][:])),
                waits=[ev_wo_ev[kk]], dma="sd%d" % par)
    ev_sd_all = [("sd0", sch.cnt["sd0"], 16), ("sd1", sch.cnt["sd1"], 16)]
    ev_attn_pe = ev_wo_mm[NSLOT * FT - 1]

    ev_xh_evac = {}
    ev_h128mm_last = None
    nch = max(1, FH // 512)
    wch = FH // nch
    for hh in range(2):
        def p4_su(eng, hh=hh):
            for ft in range(FT):
                ins = eng.tensor_scalar(
                    Fbuf[:, ft * FH:(ft + 1) * FH],
                    xnTf[:, ft * TOKF + hh * FH:ft * TOKF + hh * FH + FH],
                    su_ug_l[:, ft:ft + 1], None, ALU.mult)
            return ins
        ev_su = OP("vector", p4_su,
                   waits=[ev_attn_pe, ev_xnt.get(NSLAB - 1), ev_h128mm_last])

        ev_stg = ev_su
        src, dst = Fbuf, F2buf
        for st in range(nst1):
            d = 1 << st
            dd = F2b if st == nst1 - 1 else dst

            def p4_stage(eng, src=src, dd=dd, d=d):
                s4 = src.rearrange("p (g two w) -> p g two w", two=2, w=d * FH)
                d4 = dd.rearrange("p (g two w) -> p g two w", two=2, w=d * FH)
                eng.tensor_add(d4[:, :, 0, :], s4[:, :, 0, :], s4[:, :, 1, :])
                return eng.tensor_sub(d4[:, :, 1, :], s4[:, :, 0, :],
                                      s4[:, :, 1, :])
            ev_stg = OP("vector", p4_stage, waits=[ev_stg])
            src, dst = dst, src

        for ft in range(FT):
            for chi in range(nch):
                k = ft * nch + chi
                gk = hh * FT * nch + k

                def p4_h1(eng, ft=ft, chi=chi, k=k):
                    return eng.matmul(
                        pb[k % 2][:, 0:wch], h128_b[:],
                        F2b[:, ft * FH + chi * wch:ft * FH + chi * wch + wch],
                        start=True, stop=True)
                ev_mm = OP("tensor", p4_h1,
                           waits=[ev_stg, ev_h128, ev_xh_evac.get(gk - 2)])

                def p4_h1e(eng, ft=ft, chi=chi, k=k, hh=hh):
                    return eng.activation(
                        xnTf[:, ft * TOKF + hh * FH + chi * wch:
                             ft * TOKF + hh * FH + chi * wch + wch],
                        pb[k % 2][:, 0:wch], AF.Copy)
                ev_xh_evac[gk] = OP("scalar", p4_h1e, waits=[ev_mm, ev_su])
                ev_h128mm_last = ev_mm
    ev_xh_done = ev_xh_evac[2 * FT * nch - 1]

    ev_ug_ld = {}
    ev_ug_mm = {}
    ev_ug_ev = {}
    ev_u = {}
    nld = 0
    for it in range(IT):
        for ki, wd in ((0, wupT_d), (1, wgT_d)):
            slot = nld % 3
            pn = nld - 3
            prev_mm = ev_ug_mm.get((pn // 2, pn % 2)) if pn >= 0 else None
            ev_ug_ld[(it, ki)] = OP("gpsimd",
                (lambda eng, wd=wd, it=it, slot=slot: eng.dma_start(
                    wring[slot].rearrange("p (t c) -> p t c", t=FT),
                    wd[:, it * 128:it * 128 + 128].rearrange(
                        "(t p) c -> p t c", p=128))),
                waits=[ev_attn_pe, prev_mm], dma="wu%d" % slot)
            nld += 1

            def p4_ug(eng, it=it, ki=ki, slot=slot):
                bk0 = (0 if it % 2 == 0 else 4) + 2 * ki
                for ft in range(FT):
                    for u in range(UPC):
                        ins = eng.matmul(
                            pb[bk0 + u][:, 0:UCH],
                            wring[slot][:, ft * 128:ft * 128 + 128],
                            xnTf[:, ft * TOKF + u * UCH:ft * TOKF + u * UCH + UCH],
                            start=(ft == 0), stop=(ft == FT - 1))
                return ins
            ev_ug_mm[(it, ki)] = OP("tensor", p4_ug,
                                    waits=[ev_ug_ld[(it, ki)], ev_xh_done,
                                           ev_ug_ev.get((it - 2, ki))])

            def p4_ugev(eng, it=it, ki=ki):
                bk0 = (0 if it % 2 == 0 else 4) + 2 * ki
                vec = upvec if ki == 0 else gatevec
                func = AF.Copy if ki == 0 else AF.Silu
                dstb = A_st if ki == 0 else B_st
                for u in range(UPC):
                    ins = eng.activation(dstb[:, u * UCH:(u + 1) * UCH],
                                         pb[bk0 + u][:, 0:UCH], func,
                                         scale=vec[:, it:it + 1])
                return ins
            ev_ug_ev[(it, ki)] = OP("scalar", p4_ugev,
                                    waits=[ev_ug_mm[(it, ki)],
                                           ev_u.get(it - 1)] + ev_sd_all)

        ev_u[it] = OP("vector", (lambda eng, it=it: eng.tensor_mul(
            Uap(it, 0, TOKF), A_st[:], B_st[:])),
            waits=[ev_ug_ev[(it, 0)], ev_ug_ev[(it, 1)], ev_attn_pe])
    ev_u_all = ev_u[IT - 1]

    ev_h2_mm = {}
    ev_h2_ev = {}
    for it in range(IT):
        for u in range(UPC):
            k = it * UPC + u
            if k >= 4:
                wprev = ev_h2_ev[((k - 4) // UPC, (k - 4) % UPC)]
            else:
                wprev = ev_ug_ev[(IT - 2, 1)] if IT >= 2 else None

            def p4_h2(eng, it=it, u=u, k=k):
                return eng.matmul(pb[k % 4][:, 0:UCH], h128_b[:],
                                  Uap(it, u * UCH, UCH), start=True, stop=True)
            ev_h2_mm[(it, u)] = OP("tensor", p4_h2,
                                   waits=[ev_u[it], ev_h128, wprev])

            def p4_h2e(eng, it=it, u=u, k=k):
                return eng.activation(Uap(it, u * UCH, UCH),
                                      pb[k % 4][:, 0:UCH], AF.Copy)
            ev_h2_ev[(it, u)] = OP("scalar", p4_h2e, waits=[ev_h2_mm[(it, u)]])
    ev_h2_last = ev_h2_ev[(IT - 1, UPC - 1)]

    ev_z = None
    ev_zH = None
    for sl in range(NSL):
        sl0 = sl * CS
        ename = "vector"
        cA, cB = C1, C2

        def p4_sA(eng, sl0=sl0, cA=cA):
            q = ITQ
            for half, (ua, ub) in enumerate(((Us[0], Us[2]), (Us[1], Us[3]))):
                a = ua.rearrange("p (t c) -> p t c", c=TOKF)[:, :, sl0:sl0 + CS]
                b = ub.rearrange("p (t c) -> p t c", c=TOKF)[:, :, sl0:sl0 + CS]
                c3 = cA.rearrange("p (t c) -> p t c", c=CS)
                eng.tensor_add(c3[:, half * q:(half + 1) * q, :], a, b)
                ins = eng.tensor_sub(
                    c3[:, 2 * q + half * q:2 * q + (half + 1) * q, :], a, b)
            return ins
        ev_sA = OP(ename, p4_sA,
                   waits=[ev_h2_last if sl == 0 else ev_z,
                          ev_ug_mm[(IT - 1, 1)] if sl == 0 else None])

        ev_stg = ev_sA
        src, dst = cA, cB
        for st in range(nst2 - 2, -1, -1):
            d = 1 << st
            last = (st == 0)

            def p4_s2(eng, src=src, dst=dst, d=d, last=last, sl0=sl0):
                if not last:
                    s4 = src.rearrange("p (g two w) -> p g two w", two=2,
                                       w=d * CS)
                    d4 = dst.rearrange("p (g two w) -> p g two w", two=2,
                                       w=d * CS)
                    eng.tensor_add(d4[:, :, 0, :], s4[:, :, 0, :],
                                   s4[:, :, 1, :])
                    return eng.tensor_sub(d4[:, :, 1, :], s4[:, :, 0, :],
                                          s4[:, :, 1, :])
                s4 = src.rearrange("p (g two w) -> p g two w", two=2, w=CS)
                for uq in range(4):
                    utv = Us[uq].rearrange("p (t two c) -> p t two c", two=2,
                                           c=TOKF)[:, :, :, sl0:sl0 + CS]
                    gg = uq * (ITQ // 2)
                    eng.tensor_add(utv[:, :, 0, :],
                                   s4[:, gg:gg + ITQ // 2, 0, :],
                                   s4[:, gg:gg + ITQ // 2, 1, :])
                    ins = eng.tensor_sub(utv[:, :, 1, :],
                                         s4[:, gg:gg + ITQ // 2, 0, :],
                                         s4[:, gg:gg + ITQ // 2, 1, :])
                return ins
            ev_stg = OP(ename, p4_s2, waits=[ev_stg])
            src, dst = dst, src
        ev_z = ev_stg
        if sl == NSL // 2 - 1:
            ev_zH = ev_z

    ev_wd_ld = {}
    ev_dn_mm = {}
    ev_gs = {}
    ev_comb = {}
    ev_outs = []
    gld = 0
    gc = 0
    NEOG = FT // 4
    FHT = FH
    for ps in range(2):
        tk0 = ps * FHT
        ev_zg = ev_zH if ps == 0 else ev_z
        for eog in range(NEOG):
            g2 = ps * NEOG + eog
            for it4 in range(IT // K2):
                slot = gld % NWDR
                prev_mm = ev_dn_mm.get(gld - NWDR)
                ev_wd_ld[(ps, eog, it4)] = OP("gpsimd",
                    (lambda eng, eog=eog, it4=it4, slot=slot: eng.dma_start(
                        wdring[slot].rearrange("p (b c) -> p b c", b=K2),
                        wdT_d[it4 * K2 * 128:(it4 + 1) * K2 * 128,
                              eog * 512:eog * 512 + 512].rearrange(
                                  "(b p) c -> p b c", p=128))),
                    waits=[ev_u_all, prev_mm], dma="wd%d" % slot)

                def p4_dn(eng, eog=eog, it4=it4, slot=slot, tk0=tk0, g2=g2):
                    bk0 = (g2 % 2) * 4
                    for iti in range(K2):
                        it = it4 * K2 + iti
                        for e4 in range(4):
                            ins = eng.matmul(
                                pb[bk0 + e4][:, 0:FHT],
                                wdring[slot][:, iti * 512 + e4 * 128:
                                             iti * 512 + e4 * 128 + 128],
                                Uap(it, tk0, FHT),
                                start=(it == 0), stop=(it == IT - 1))
                    return ins
                ws = [ev_wd_ld[(ps, eog, it4)]]
                if it4 == 0:
                    ws += [ev_zg]
                    ws += [ev_gs.get((g2 - 2, kkk)) for kkk in range(4)]
                ev_dn_mm[gld] = OP("tensor", p4_dn, waits=ws)
                gld += 1

            for e4 in range(4):
                eo = eog * 4 + e4
                par = gc % 2
                bk = (g2 % 2) * 4 + e4
                prev2 = ev_comb.get(gc - 2)
                prev_out = ev_outs[gc - 1] if gc >= 2 else None

                ev_g = OP("scalar", (lambda eng, bk=bk, eo=eo, par=par:
                    eng.activation(g_st[:, par * FHT:(par + 1) * FHT],
                                   pb[bk][:, 0:FHT], AF.Silu,
                                   scale=dnvec[:, eo:eo + 1])),
                    waits=[ev_dn_mm[gld - 1], prev2])
                ev_gs[(g2, e4)] = ev_g

                ev_ss = OP("scalar", (lambda eng, eo=eo, tk0=tk0, par=par:
                    eng.dma_start(ss_st[:, par * FHT:(par + 1) * FHT],
                                  s_dram[eo * 128:(eo + 1) * 128,
                                         tk0:tk0 + FHT])),
                    waits=ev_sd_all + [prev2, ev_zg], dma="st%d" % par)

                ev_xf = OP("scalar", (lambda eng, eo=eo, tk0=tk0, par=par:
                    eng.dma_start(xf_st[:, par * FHT:(par + 1) * FHT],
                                  xfT_d[eo * 128:(eo + 1) * 128,
                                        tk0:tk0 + FHT])),
                    waits=[prev2, ev_zg], dma="st%d" % par)

                def p4_comb(eng, par=par):
                    sch.chain("vector", eng, eng.tensor_mul(
                        o_st[:, par * FHT:(par + 1) * FHT],
                        g_st[:, par * FHT:(par + 1) * FHT],
                        ss_st[:, par * FHT:(par + 1) * FHT]))
                    return eng.tensor_add(
                        res_st[:, par * FHT:(par + 1) * FHT],
                        o_st[:, par * FHT:(par + 1) * FHT],
                        xf_st[:, par * FHT:(par + 1) * FHT])
                ev_cb = OP("vector", p4_comb,
                           waits=[ev_g, ev_ss, ev_xf, prev_out])
                ev_comb[gc] = ev_cb

                ev_outs.append(OP("sync", (lambda eng, eo=eo, tk0=tk0,
                                           par=par:
                    eng.dma_start(out_d[eo * 128:(eo + 1) * 128,
                                        tk0:tk0 + FHT],
                                  res_st[:, par * FHT:(par + 1) * FHT])),
                    waits=[ev_cb], dma="outd"))
                gc += 1

    n_out = sch.cnt["outd"]
    sch.ops["sync"].append(
        ((lambda eng: eng.nop()), (("outd", n_out, 16),), None))

    blk = nc.Block()
    blk.__enter__()
    sch.emit(blk)
    blk.__exit__(None, None, None)
    return nc


def _host_in_maps(cfg, inputs):
    x = np.ascontiguousarray(inputs["x"], dtype=np.float32)
    h128 = build_h128()
    slopes_all = np.asarray(cfg.slopes, np.float32)
    wqT = np.ascontiguousarray(inputs["Wq"].T)
    wkT = np.ascontiguousarray(inputs["Wk"].T)
    wvT = np.ascontiguousarray(inputs["Wv"].T)
    woT = np.ascontiguousarray(inputs["Wo"].T)
    wupT = np.ascontiguousarray(inputs["W_up"].T)
    wgT = np.ascontiguousarray(inputs["W_gate"].T)
    wdT = np.ascontiguousarray(inputs["W_down"].T)

    in_maps = []
    for c in range(8):
        b, r = c // 4, c % 4
        grps = [cfg.grp(j, r) for j in range(cfg.NSLOT)]
        cols = np.concatenate([np.arange(g * cfg.QG, (g + 1) * cfg.QG)
                               for g in grps])
        xb = x[b]
        in_maps.append({
            "xT": np.ascontiguousarray(xb.T),
            "xfT": np.ascontiguousarray(xb.T[:, cols]),
            "wqT": wqT, "wkT": wkT, "wvT": wvT, "woT": woT,
            "bq": np.ascontiguousarray(inputs["bq"]),
            "bk": np.ascontiguousarray(inputs["bk"]),
            "bv": np.ascontiguousarray(inputs["bv"]),
            "bo": np.ascontiguousarray(inputs["bo"]),
            "slopes": slopes_all,
            "mwin": build_mwin(cfg, r),
            "su_ug": np.ascontiguousarray(inputs["su_ug"]),
            "su_dp": np.ascontiguousarray(inputs["su_dp"]),
            "wsc": np.ascontiguousarray(inputs["wsc"]),
            "wdp": np.ascontiguousarray(inputs["wdp"]),
            "wupT": wupT, "wgT": wgT, "wdT": wdT,
            "h128": h128,
        })
    return in_maps


def run(cfg, inputs):
    nc = build_program(cfg)
    in_maps = _host_in_maps(cfg, inputs)
    res = run_bass_kernel_spmd(nc, in_maps, core_ids=list(range(8)))
    out = np.empty((2, cfg.S, cfg.E), np.float32)
    for c in range(8):
        b, r = c // 4, c % 4
        o = np.asarray(res.results[c]["out"])
        for j in range(cfg.NSLOT):
            g = cfg.grp(j, r)
            out[b, g * cfg.QG:(g + 1) * cfg.QG, :] = \
                o[:, j * cfg.QG:(j + 1) * cfg.QG].T
    return out


def kernel(L0_attn_x, L0_attn_Wq, L0_attn_bq, L0_attn_Wk, L0_attn_bk,
           L0_attn_Wv, L0_attn_bv, L0_attn_Wo, L0_attn_bo,
           L0_ffn_SU_ug, L0_ffn_SU_dp, L0_ffn_Wscale_ug, L0_ffn_Wscale_dp,
           L0_ffn_W_up, L0_ffn_W_gate, L0_ffn_W_down):
    cfg = Cfg(S=4096, E=2048, I=8192)
    inputs = dict(
        x=np.asarray(L0_attn_x, np.float32),
        Wq=np.asarray(L0_attn_Wq, np.float32),
        Wk=np.asarray(L0_attn_Wk, np.float32),
        Wv=np.asarray(L0_attn_Wv, np.float32),
        Wo=np.asarray(L0_attn_Wo, np.float32),
        bq=np.asarray(L0_attn_bq, np.float32),
        bk=np.asarray(L0_attn_bk, np.float32),
        bv=np.asarray(L0_attn_bv, np.float32),
        bo=np.asarray(L0_attn_bo, np.float32),
        su_ug=np.asarray(L0_ffn_SU_ug, np.float32),
        su_dp=np.asarray(L0_ffn_SU_dp, np.float32),
        wsc=np.asarray(L0_ffn_Wscale_ug, np.float32),
        wdp=np.asarray(L0_ffn_Wscale_dp, np.float32),
        W_up=np.asarray(L0_ffn_W_up, np.float32),
        W_gate=np.asarray(L0_ffn_W_gate, np.float32),
        W_down=np.asarray(L0_ffn_W_down, np.float32),
    )
    return run(cfg, inputs)
